# version 08
# baseline (speedup 1.0000x reference)
import contextlib
import numpy as np
import ml_dtypes
import concourse.bass as bass
import concourse.mybir as mybir
from concourse.bass_utils import run_bass_kernel_spmd

F32 = mybir.dt.float32
BF16 = mybir.dt.bfloat16
ALU = mybir.AluOpType
ACTF = mybir.ActivationFunctionType
AXX = mybir.AxisListType.X

P = 128
D = 1024
ET = 8
DFF = 4096
FT = 32
T = 2048
TT = T // P
C2 = 256
NC2 = T // C2
C3 = 256
NC3 = T // C3
FC = 256
NFC = DFF // FC
BN_EPS = 1e-5
LN_EPS = 1e-5
FFN_EPS = 1e-5

W_ORD = ("q", "k", "v")
N_CORES = 8
DBG = False


class Marks:

    def __init__(self):
        self.d = {}
        self.check = False

    def put(self, key, val):
        if self.check and key in self.d and self.d[key] != val:
            raise AssertionError(f"mark drift {key}: pass1={self.d[key]} pass2={val}")
        self.d[key] = val
        return val

    def get(self, key):
        return self.d.get(key, 0)


class Sem:
    def __init__(self, stack, nc, name):
        self.h = stack.enter_context(nc.semaphore(name))
        self.v = 0

    def inc(self, instr, n=1):
        instr.then_inc(self.h, n)
        self.v += n
        return self.v

    def wait(self, eng, val):
        if val > 0:
            eng.wait_ge(self.h, val)


def build_nc(n_cores, marks):
    NTOT = T * n_cores
    M = marks

    nc = bass.Bass("TRN2", target_bir_lowering=False, debug=False,
                   detect_race_conditions=False)

    x_io = nc.dram_tensor("x", [T, D], F32, kind="ExternalInput").ap()
    w_io = {b: nc.dram_tensor(f"w{b}", [D, D], F32, kind="ExternalInput").ap()
            for b in ("q", "k", "v", "p")}
    w1_io = nc.dram_tensor("w1", [DFF, D], F32, kind="ExternalInput").ap()
    w2_io = nc.dram_tensor("w2", [D, DFF], F32, kind="ExternalInput").ap()
    vec_io = {nm: nc.dram_tensor(nm, [P, 8], F32, kind="ExternalInput").ap()
              for nm in ("gq", "bq", "gk", "bk", "gv", "bv", "bp", "b2v")}
    b1v_io = nc.dram_tensor("b1v", [P, 32], F32, kind="ExternalInput").ap()
    eyef_io = nc.dram_tensor("eyef", [P, P], F32, kind="ExternalInput").ap()
    onesq_io = nc.dram_tensor("onesq", [P, P], F32, kind="ExternalInput").ap()
    eyeb_io = nc.dram_tensor("eyeb", [P, P], BF16, kind="ExternalInput").ap()
    ones_io = nc.dram_tensor("ones128", [P, 1], F32, kind="ExternalInput").ap()
    out_io = nc.dram_tensor("out", [T, D], F32, kind="ExternalOutput").ap()

    h_sp = nc.dram_tensor("h_sp", [3, NC3, P, ET * C3], F32)
    wq1t_h = nc.dram_tensor("wq1t", [FT, P, D], BF16)
    wq2t_h = nc.dram_tensor("wq2t", [FT, P, ET, P], BF16)
    ar1_i = nc.dram_tensor("ar1_i", [P, 48], F32)
    ar1_o = nc.dram_tensor("ar1_o", [P, 48], F32)
    ar2_i = nc.dram_tensor("ar2_i", [P, 1024], F32)
    ar2_o = nc.dram_tensor("ar2_o", [P, 1024], F32)
    if DBG:
        dbgb = nc.dram_tensor("dbgb", [4, P, 2048], BF16, kind="ExternalOutput")
        dbgf = nc.dram_tensor("dbgf", [5, P, 2048], F32, kind="ExternalOutput")
        dbgx = nc.dram_tensor("dbgx", [2, P, 2048], BF16, kind="ExternalOutput")
        dbgh = nc.dram_tensor("dbgh", [3, P, 2048], F32, kind="ExternalOutput")
        dbgc = nc.dram_tensor("dbgc", [P, 768], F32, kind="ExternalOutput")
        dbgw = nc.dram_tensor("dbgw", [6, P, 1024], BF16, kind="ExternalOutput")

    stack = contextlib.ExitStack()
    sb = lambda name, shape, dt: stack.enter_context(nc.sbuf_tensor(name, shape, dt))
    psb = lambda name, shape: stack.enter_context(nc.psum_tensor(name, shape, F32))

    FA = sb("FA", [P, TT * 1024], F32)
    FB = sb("FB", [P, 12288], F32)
    B1 = sb("B1", [P, TT * 1024], BF16)
    B2 = sb("B2", [P, TT * 1024], BF16)
    SL = sb("SL", [P, 8192], BF16)
    PF = sb("PF", [P, 768], F32)
    ONESQ = sb("ONESQ", [P, P], F32)
    PB = sb("PB", [P, 2304], BF16)
    pb = [psb(f"pb{i}", [P, 512]) for i in range(8)]

    pfv = {"gq": 0, "bq": 8, "gk": 16, "bk": 24, "gv": 32, "bv": 40,
           "bp": 48, "b2v": 56}
    PF_B1V = 64
    PF_EYE = 96
    PF_ONE = 224
    PF_ST = 232
    PF_SR = 280
    PF_AC = 328
    PF_WK = 376
    PF_ZS = 408
    PF_Z2 = 440
    PF_M = 456
    PF_R = 472
    PF_G = 488
    PF_ACb = 504
    PB_EYEB = 0
    PB_ST = 128
    PB_H1H = 128
    PB_H1L = 1152

    sems = {}
    for nm in ("s_x", "s_vec", "s_wst", "s_h2", "s_h3", "s_xr", "s_fsl",
               "s_wg", "g_h", "g_wb", "g_ar", "g_out",
               "cc", "c_t", "c_a", "c_v", "c_g"):
        sems[nm] = Sem(stack, nc, nm)
    s = sems

    blk = stack.enter_context(nc.Block())

    eyef = PF[:, PF_EYE:PF_EYE + P]
    eyeb = PB[:, PB_EYEB:PB_EYEB + P]
    ones128 = PF[:, PF_ONE:PF_ONE + 1]

    def bar(name):
        for nm in ("c_t", "c_a", "c_v", "c_g"):
            M.put(("bar", name, nm), s[nm].v)

    def wait_bar(e, name, extra=()):
        for nm in ("c_t", "c_a", "c_v", "c_g"):
            s[nm].wait(e, M.get(("bar", name, nm)))
        for nm, val in extra:
            s[nm].wait(e, val)

    @blk.sync
    def _(e):
        M.put("x_ld", s["s_x"].inc(
            e.dma_start(out=FA[:, :].rearrange("p (tt d) -> p tt d", tt=TT),
                        in_=x_io.rearrange("(tt p) d -> p tt d", p=P)), 16))
        for nm, col in pfv.items():
            s["s_vec"].inc(e.dma_start(out=PF[:, col:col + 8], in_=vec_io[nm]), 16)
        s["s_vec"].inc(e.dma_start(out=PF[:, PF_B1V:PF_B1V + 32], in_=b1v_io), 16)
        s["s_vec"].inc(e.dma_start(out=PF[:, PF_EYE:PF_EYE + P], in_=eyef_io), 16)
        s["s_vec"].inc(e.dma_start(out=PF[:, PF_ONE:PF_ONE + 1], in_=ones_io), 16)
        s["s_vec"].inc(e.dma_start(out=ONESQ[:, :], in_=onesq_io), 16)
        M.put("vec_ld", s["s_vec"].inc(
            e.dma_start(out=PB[:, PB_EYEB:PB_EYEB + P], in_=eyeb_io), 16))

    NG0 = ET * (TT // 4)

    @blk.tensor
    def _(e):
        s["s_x"].wait(e, M.get("x_ld"))
        s["s_vec"].wait(e, M.get("vec_ld"))
        for g in range(NG0):
            kt, j = divmod(g, TT // 4)
            bank = pb[g % 2]
            if g >= 2:
                s["c_a"].wait(e, M.get(("p0a", g - 2)))
                s["c_v"].wait(e, M.get(("p0v", g - 2)))
            for q in range(4):
                tt = 4 * j + q
                ins = e.matmul(bank[:, q * P:(q + 1) * P],
                               FA[:, tt * 1024 + kt * P: tt * 1024 + (kt + 1) * P],
                               eyef, is_transpose=True, start=(q == 0), stop=(q == 3))
            M.put(("p0t", g), s["c_t"].inc(e.drain()))

    @blk.scalar
    def _(e):
        for g in range(NG0):
            kt, j = divmod(g, TT // 4)
            bank = pb[g % 2]
            s["c_t"].wait(e, M.get(("p0t", g)))
            ins = e.activation(out=B1[:, kt * T + 4 * j * P: kt * T + (4 * j + 4) * P],
                               in_=bank[:], func=ACTF.Copy)
            M.put(("p0a", g), s["c_a"].inc(ins))

    @blk.vector
    def _(e):
        for g in range(NG0):
            kt, j = divmod(g, TT // 4)
            bank = pb[g % 2]
            s["c_t"].wait(e, M.get(("p0t", g)))
            s["c_a"].wait(e, M.get(("p0a", g)))
            ins = e.scalar_tensor_tensor(
                out=B2[:, kt * T + 4 * j * P: kt * T + (4 * j + 4) * P],
                in0=bank[:], scalar=0.0,
                in1=B1[:, kt * T + 4 * j * P: kt * T + (4 * j + 4) * P],
                op0=ALU.add, op1=ALU.subtract)
            M.put(("p0v", g), s["c_v"].inc(ins))

    NSL = 3 * ET

    @blk.sync
    def _(e):
        for i in range(NSL):
            b, et = divmod(i, ET)
            if i >= 2:
                s["c_t"].wait(e, M.get(("p1wt", i - 2)))
            ins = e.dma_start(out=FB[:, (i % 2) * 1024:(i % 2) * 1024 + 1024],
                              in_=w_io[W_ORD[b]][et * P:(et + 1) * P, :])
            M.put(("p1wl", i), s["s_wst"].inc(ins, 16))

    @blk.tensor
    def _(e):
        for i in range(NSL + 1):
            if i < NSL:
                sl = i % 2
                s["s_wst"].wait(e, M.get(("p1wl", i)))
                for half in range(2):
                    bank = pb[4 + half]
                    if i >= 1:
                        s["c_a"].wait(e, M.get(("p1wa", i - 1, half)))
                        s["c_v"].wait(e, M.get(("p1wv", i - 1, half)))
                    for q in range(4):
                        kt = half * 4 + q
                        ins = e.matmul(bank[:, q * P:(q + 1) * P],
                                       FB[:, sl * 1024 + kt * P: sl * 1024 + (kt + 1) * P],
                                       eyef, is_transpose=True, start=(q == 0), stop=(q == 3))
                    M.put(("p1wt2", i, half), s["c_t"].inc(e.drain()))
                M.put(("p1wt", i), s["c_t"].v)
            if i >= 1:
                j = i - 1
                sl = j % 2
                s["c_a"].wait(e, M.get(("p1wa", j, 1)))
                s["c_v"].wait(e, M.get(("p1wv", j, 1)))
                passes = ([(0, kt, 0) for kt in range(ET)]
                          + [(1, kt, 0) for kt in range(ET)]
                          + [(0, kt, 1) for kt in range(ET)])
                for h in range(2):
                    if j == 0 and h == 0:
                        s["c_a"].wait(e, M.get(("p0a", NG0 - 1)))
                        s["c_v"].wait(e, M.get(("p0v", NG0 - 1)))
                    if j >= 1:
                        s["c_a"].wait(e, M.get(("p1ha", j - 1, h * 2 + 1)))
                        s["c_v"].wait(e, M.get(("p1hv", j - 1, h * 2 + 1)))
                    for ip, (wl, kt, xl) in enumerate(passes):
                        wt = SL[:, 2048 * wl + sl * 1024: 2048 * wl + (sl + 1) * 1024]
                        xs = B1 if xl == 0 else B2
                        for nt in (h * 2, h * 2 + 1):
                            ins = e.matmul(pb[nt][:],
                                           wt[:, kt * P:(kt + 1) * P],
                                           xs[:, kt * T + nt * 512: kt * T + (nt + 1) * 512],
                                           start=(ip == 0), stop=(ip == len(passes) - 1))
                    M.put(("p1mmh", j, h), s["c_t"].inc(e.drain()))
                M.put(("p1mm", j), s["c_t"].v)

    @blk.scalar
    def _(e):
        for i in range(NSL):
            b, et = divmod(i, ET)
            sl = i % 2
            for half in range(2):
                bank = pb[4 + half]
                s["c_t"].wait(e, M.get(("p1wt2", i, half)))
                ins = e.activation(
                    out=SL[:, sl * 1024 + half * 512: sl * 1024 + half * 512 + 512],
                    in_=bank[:], func=ACTF.Copy)
                M.put(("p1wa", i, half), s["c_a"].inc(ins))
            for nt in range(4):
                s["c_t"].wait(e, M.get(("p1mmh", i, nt // 2)))
                if i >= 1:
                    s["g_h"].wait(e, M.get(("p1sp", i - 1, nt)))
                ins = e.activation(
                    out=FB[:, 2048 + nt * 512: 2048 + nt * 512 + 512],
                    in_=pb[nt][:], func=ACTF.Copy,
                    accum_out=PF[:, PF_ACb + et * 4 + nt: PF_ACb + et * 4 + nt + 1])
                M.put(("p1ha", i, nt), s["c_a"].inc(ins))
            M.put(("p1ea", i), s["c_a"].v)

    @blk.vector
    def _(e):
        for i in range(NSL):
            b, et = divmod(i, ET)
            sl = i % 2
            for half in range(2):
                bank = pb[4 + half]
                s["c_t"].wait(e, M.get(("p1wt2", i, half)))
                s["c_a"].wait(e, M.get(("p1wa", i, half)))
                ins = e.scalar_tensor_tensor(
                    out=SL[:, 2048 + sl * 1024 + half * 512: 2048 + sl * 1024 + half * 512 + 512],
                    in0=bank[:], scalar=0.0,
                    in1=SL[:, sl * 1024 + half * 512: sl * 1024 + half * 512 + 512],
                    op0=ALU.add, op1=ALU.subtract)
                M.put(("p1wv", i, half), s["c_v"].inc(ins))
            for nt in range(4):
                s["c_a"].wait(e, M.get(("p1ha", i, nt)))
                stg = FB[:, 2048 + nt * 512: 2048 + nt * 512 + 512]
                ins = e.scalar_tensor_tensor(
                    out=FB[:, 4096:4608], in0=stg, scalar=0.0,
                    in1=stg, op0=ALU.add, op1=ALU.mult,
                    accum_out=PF[:, PF_ACb + 32 + et * 4 + nt: PF_ACb + 32 + et * 4 + nt + 1])
                M.put(("p1hv", i, nt), s["c_v"].inc(ins))
            M.put(("p1ev", i), s["c_v"].v)
            if et == ET - 1:
                s["c_a"].wait(e, M.get(("p1ea", i)))
                e.drain()
                ins = e.tensor_reduce(
                    out=PF[:, PF_ST + b * 16: PF_ST + b * 16 + 8]
                        .rearrange("p (e o) -> p e o", o=1),
                    in_=PF[:, PF_ACb:PF_ACb + 32].rearrange("p (e n) -> p e n", n=4),
                    axis=AXX, op=ALU.add)
                s["c_v"].inc(ins)
                ins = e.tensor_reduce(
                    out=PF[:, PF_ST + b * 16 + 8: PF_ST + b * 16 + 16]
                        .rearrange("p (e o) -> p e o", o=1),
                    in_=PF[:, PF_ACb + 32:PF_ACb + 64].rearrange("p (e n) -> p e n", n=4),
                    axis=AXX, op=ALU.add)
                M.put(("p1st", b), s["c_v"].inc(ins))

    @blk.gpsimd
    def _(e):
        for i in range(NSL):
            b, et = divmod(i, ET)
            for nt in range(4):
                s["c_a"].wait(e, M.get(("p1ha", i, nt)))
                s["c_v"].wait(e, M.get(("p1hv", i, nt)))
                ins = e.dma_start(
                    out=h_sp.ap()[b, 2 * nt:2 * nt + 2, :, et * C3:(et + 1) * C3]
                        .rearrange("c p n -> p c n"),
                    in_=FB[:, 2048 + nt * 512: 2048 + nt * 512 + 512]
                        .rearrange("p (c n) -> p c n", c=2))
                M.put(("p1sp", i, nt), s["g_h"].inc(ins, 16))

    @blk.gpsimd
    def _(e):
        s["c_v"].wait(e, M.get(("p1st", 2)))
        s["g_ar"].inc(e.dma_start(out=ar1_i.ap(), in_=PF[:, PF_ST:PF_ST + 48]), 16)
        e.wait_ge(s["g_ar"].h, s["g_ar"].v)
        s["cc"].inc(e.collective_compute(
            "AllReduce", ALU.add, replica_groups=[list(range(n_cores))],
            ins=[ar1_i.ap()], outs=[ar1_o.ap()]), 1)
        e.wait_ge(s["cc"].h, s["cc"].v)
        M.put("ar1_done", s["g_ar"].inc(
            e.dma_start(out=PF[:, PF_SR:PF_SR + 48], in_=ar1_o.ap()), 16))

    @blk.tensor
    def _(e):
        bar("p1end")

    @blk.sync
    def _(e):
        wait_bar(e, "p1end", extra=[("g_h", M.get(("p1sp", NSL - 1, 3)))])
        for c in range(FT):
            if c >= 2:
                s["c_v"].wait(e, M.get(("g1r", c - 2)))
            ins = e.dma_start(out=FB[:, (c % 2) * 1024:(c % 2) * 1024 + 1024],
                              in_=w1_io[c * P:(c + 1) * P, :])
            M.put(("g1l", c), s["s_wg"].inc(ins, 16))
        for c in range(ET):
            if c >= 2:
                s["c_v"].wait(e, M.get(("g2r", c - 2)))
            ins = e.dma_start(out=FB[:, 2048 + (c % 2) * 4096: 2048 + (c % 2) * 4096 + 4096],
                              in_=w2_io[c * P:(c + 1) * P, :])
            M.put(("g2l", c), s["s_wg"].inc(ins, 16))

    @blk.vector
    def _(e):
        for c in range(FT):
            s["s_wg"].wait(e, M.get(("g1l", c)))
            ins = e.tensor_reduce(out=PF[:, PF_ACb + c:PF_ACb + c + 1],
                                  in_=FB[:, (c % 2) * 1024:(c % 2) * 1024 + 1024],
                                  axis=AXX, op=ALU.add, apply_absolute_value=True)
            M.put(("g1r", c), s["c_v"].inc(ins))
        for c in range(ET):
            s["s_wg"].wait(e, M.get(("g2l", c)))
            ins = e.tensor_reduce(out=PF[:, PF_ACb + 32 + c:PF_ACb + 33 + c],
                                  in_=FB[:, 2048 + (c % 2) * 4096: 2048 + (c % 2) * 4096 + 4096],
                                  axis=AXX, op=ALU.add, apply_absolute_value=True)
            M.put(("g2r", c), s["c_v"].inc(ins))
        e.drain()
        ins = e.tensor_reduce(out=PF[:, PF_G:PF_G + 1],
                              in_=PF[:, PF_ACb:PF_ACb + 32]
                                  .rearrange("p (o n) -> p o n", o=1),
                              axis=AXX, op=ALU.add)
        s["c_v"].inc(ins)
        ins = e.tensor_reduce(out=PF[:, PF_G + 1:PF_G + 2],
                              in_=PF[:, PF_ACb + 32:PF_ACb + 40]
                                  .rearrange("p (o n) -> p o n", o=1),
                              axis=AXX, op=ALU.add)
        M.put("gsum", s["c_v"].inc(ins))

    @blk.tensor
    def _(e):
        s["c_v"].wait(e, M.get("gsum"))
        ins = e.matmul(pb[6][:, 0:2], ONESQ[:, :], PF[:, PF_G:PF_G + 2],
                       start=True, stop=True)
        M.put("gtot", s["c_t"].inc(e.drain()))

    @blk.scalar
    def _(e):
        s["c_t"].wait(e, M.get("gtot"))
        ins = e.activation(out=PF[:, PF_G + 4:PF_G + 6], in_=pb[6][:, 0:2],
                           func=ACTF.Copy, scale=0.5 / (DFF * D), bias=0.5 * FFN_EPS)
        M.put("gthr", s["c_a"].inc(ins))

    @blk.vector
    def _(e):
        s["c_a"].wait(e, M.get("gthr"))
        ins = e.tensor_scalar(out=PF[:, PF_G + 6:PF_G + 8],
                              in0=PF[:, PF_G + 4:PF_G + 6],
                              scalar1=-1.0, scalar2=None, op0=ALU.mult)
        M.put("gneg", s["c_v"].inc(ins))
        e.drain()

    @blk.sync
    def _(e):
        s["c_v"].wait(e, M.get("gneg"))
        for c in range(FT):
            if c >= 2:
                s["c_v"].wait(e, M.get(("q1q", c - 2)))
            ins = e.dma_start(out=FB[:, (c % 2) * 1024:(c % 2) * 1024 + 1024],
                              in_=w1_io[c * P:(c + 1) * P, :])
            M.put(("q1l", c), s["s_wg"].inc(ins, 16))
        for c in range(ET):
            if c >= 2:
                s["c_v"].wait(e, M.get(("q2q", c - 2, 3)))
            ins = e.dma_start(out=FB[:, 2048 + (c % 2) * 4096: 2048 + (c % 2) * 4096 + 4096],
                              in_=w2_io[c * P:(c + 1) * P, :])
            M.put(("q2l", c), s["s_wg"].inc(ins, 16))

    @blk.vector
    def _(e):
        for c in range(FT):
            s["s_wg"].wait(e, M.get(("q1l", c)))
            src = FB[:, (c % 2) * 1024:(c % 2) * 1024 + 1024]
            ins = e.tensor_scalar(out=FB[:, 10240:11264], in0=src,
                                  scalar1=PF[:, PF_G + 6:PF_G + 7], scalar2=None,
                                  op0=ALU.is_lt)
            s["c_v"].inc(ins)
            if c >= 1:
                s["c_t"].wait(e, M.get(("q1t", c - 1)))
            ins = e.scalar_tensor_tensor(
                out=FB[:, 11264:12288],
                in0=src, scalar=PF[:, PF_G + 4:PF_G + 5], op0=ALU.is_gt,
                in1=FB[:, 10240:11264], op1=ALU.subtract)
            M.put(("q1q", c), s["c_v"].inc(ins))
        for c in range(ET):
            s["s_wg"].wait(e, M.get(("q2l", c)))
            src = FB[:, 2048 + (c % 2) * 4096: 2048 + (c % 2) * 4096 + 4096]
            for hh in range(4):
                ins = e.tensor_scalar(out=FB[:, 10240:11264],
                                      in0=src[:, hh * 1024:(hh + 1) * 1024],
                                      scalar1=PF[:, PF_G + 7:PF_G + 8], scalar2=None,
                                      op0=ALU.is_lt)
                s["c_v"].inc(ins)
                g = c * 4 + hh
                if g >= 1:
                    s["c_t"].wait(e, M.get(("q2t", (g - 1) // 4, (g - 1) % 4)))
                else:
                    s["c_t"].wait(e, M.get(("q1t", FT - 1)))
                ins = e.scalar_tensor_tensor(
                    out=FB[:, 11264:12288],
                    in0=src[:, hh * 1024:(hh + 1) * 1024],
                    scalar=PF[:, PF_G + 5:PF_G + 6], op0=ALU.is_gt,
                    in1=FB[:, 10240:11264], op1=ALU.subtract)
                M.put(("q2q", c, hh), s["c_v"].inc(ins))

    @blk.tensor
    def _(e):
        for c in range(FT):
            s["c_v"].wait(e, M.get(("q1q", c)))
            for half in range(2):
                bank = pb[6 + half]
                if c == 0:
                    s["c_a"].wait(e, M.get("gthr"))
                if c >= 1:
                    s["c_a"].wait(e, M.get(("wba1", c - 1, half)))
                for q in range(4):
                    dt = half * 4 + q
                    ins = e.matmul(bank[:, q * P:(q + 1) * P],
                                   FB[:, 11264 + dt * P: 11264 + (dt + 1) * P],
                                   eyef, is_transpose=True, start=(q == 0), stop=(q == 3))
                M.put(("q1t2", c, half), s["c_t"].inc(e.drain()))
            M.put(("q1t", c), s["c_t"].v)
        for c in range(ET):
            for hh in range(4):
                g = c * 4 + hh
                s["c_v"].wait(e, M.get(("q2q", c, hh)))
                for half in range(2):
                    bank = pb[6 + half]
                    if g >= 1:
                        s["c_a"].wait(e, M.get(("wba2", (g - 1) // 4, (g - 1) % 4, half)))
                    else:
                        s["c_a"].wait(e, M.get(("wba1", FT - 1, half)))
                    for q in range(4):
                        ft = half * 4 + q
                        ins = e.matmul(bank[:, q * P:(q + 1) * P],
                                       FB[:, 11264 + ft * P: 11264 + (ft + 1) * P],
                                       eyef, is_transpose=True,
                                       start=(q == 0), stop=(q == 3))
                    M.put(("q2t2", c, hh, half), s["c_t"].inc(e.drain()))
                M.put(("q2t", c, hh), s["c_t"].v)

    @blk.scalar
    def _(e):
        for c in range(FT):
            for half in range(2):
                s["c_t"].wait(e, M.get(("q1t2", c, half)))
                if c >= 2:
                    s["g_wb"].wait(e, M.get(("wb1", c - 2)))
                ins = e.activation(
                    out=PB[:, 128 + (c % 2) * 1024 + half * 512:
                           128 + (c % 2) * 1024 + half * 512 + 512],
                    in_=pb[6 + half][:], func=ACTF.Copy)
                M.put(("wba1", c, half), s["c_a"].inc(ins))
        for c in range(ET):
            for hh in range(4):
                g = c * 4 + hh
                for half in range(2):
                    s["c_t"].wait(e, M.get(("q2t2", c, hh, half)))
                    if g >= 2:
                        s["g_wb"].wait(e, M.get(("wb2", (g - 2) // 4, (g - 2) % 4)))
                    ins = e.activation(
                        out=PB[:, 128 + (g % 2) * 1024 + half * 512:
                               128 + (g % 2) * 1024 + half * 512 + 512],
                        in_=pb[6 + half][:], func=ACTF.Copy)
                    M.put(("wba2", c, hh, half), s["c_a"].inc(ins))

    @blk.gpsimd
    def _(e):
        for c in range(FT):
            s["c_a"].wait(e, M.get(("wba1", c, 1)))
            ins = e.dma_start(out=wq1t_h.ap()[c, :, :],
                              in_=PB[:, 128 + (c % 2) * 1024: 128 + (c % 2) * 1024 + 1024])
            M.put(("wb1", c), s["g_wb"].inc(ins, 16))
        for c in range(ET):
            for hh in range(4):
                g = c * 4 + hh
                s["c_a"].wait(e, M.get(("wba2", c, hh, 1)))
                ins = e.dma_start(
                    out=wq2t_h.ap()[hh * 8:(hh + 1) * 8, :, c, :]
                        .rearrange("f p dl -> p f dl"),
                    in_=PB[:, 128 + (g % 2) * 1024: 128 + (g % 2) * 1024 + 1024]
                        .rearrange("p (f dl) -> p f dl", f=8))
                M.put(("wb2", c, hh), s["g_wb"].inc(ins, 16))
        M.put("prep_done", s["g_wb"].v)

    @blk.vector
    def _(e):
        bar("p15end")

    @blk.vector
    def _(e):
        s["g_ar"].wait(e, M.get("ar1_done"))
        for b in range(3):
            if b >= 1:
                s["c_a"].wait(e, M.get(("bnac", b - 1)))
            sr = PF[:, PF_SR + b * 16:PF_SR + b * 16 + 8]
            qr = PF[:, PF_SR + b * 16 + 8:PF_SR + b * 16 + 16]
            g_v = PF[:, pfv[("gq", "gk", "gv")[b]]:pfv[("gq", "gk", "gv")[b]] + 8]
            e.tensor_scalar(out=PF[:, PF_WK:PF_WK + 8], in0=sr,
                            scalar1=1.0 / NTOT, scalar2=None, op0=ALU.mult)
            e.tensor_scalar(out=PF[:, PF_WK + 8:PF_WK + 16], in0=qr,
                            scalar1=1.0 / NTOT, scalar2=None, op0=ALU.mult)
            e.drain()
            e.tensor_mul(out=PF[:, PF_WK + 16:PF_WK + 24],
                         in0=PF[:, PF_WK:PF_WK + 8], in1=PF[:, PF_WK:PF_WK + 8])
            e.drain()
            ins = e.scalar_tensor_tensor(out=PF[:, PF_WK + 16:PF_WK + 24],
                                         in0=PF[:, PF_WK + 8:PF_WK + 16],
                                         scalar=BN_EPS, op0=ALU.add,
                                         in1=PF[:, PF_WK + 16:PF_WK + 24],
                                         op1=ALU.subtract)
            M.put(("bnv", b), s["c_v"].inc(ins))
            s["c_a"].wait(e, M.get(("bnr", b)))
            e.reciprocal(out=PF[:, PF_WK + 8:PF_WK + 16],
                         in_=PF[:, PF_WK + 8:PF_WK + 16])
            e.drain()
            e.tensor_mul(out=PF[:, PF_WK + 24:PF_WK + 32],
                         in0=g_v, in1=PF[:, PF_WK + 8:PF_WK + 16])
            e.drain()
            e.tensor_mul(out=PF[:, PF_WK + 16:PF_WK + 24],
                         in0=PF[:, PF_WK + 24:PF_WK + 32], in1=PF[:, PF_WK:PF_WK + 8])
            e.drain()
            beta = PF[:, pfv[("bq", "bk", "bv")[b]]:pfv[("bq", "bk", "bv")[b]] + 8]
            ins = e.tensor_sub(out=PF[:, PF_WK + 16:PF_WK + 24],
                               in0=beta, in1=PF[:, PF_WK + 16:PF_WK + 24])
            M.put(("bnw", b), s["c_v"].inc(ins))

    @blk.scalar
    def _(e):
        for b in range(3):
            s["c_v"].wait(e, M.get(("bnv", b)))
            ins = e.activation(out=PF[:, PF_WK + 8:PF_WK + 16],
                               in_=PF[:, PF_WK + 16:PF_WK + 24],
                               func=ACTF.Sqrt)
            M.put(("bnr", b), s["c_a"].inc(ins))
            s["c_v"].wait(e, M.get(("bnw", b)))
            e.activation(out=PF[:, PF_AC + b * 16:PF_AC + b * 16 + 8],
                         in_=PF[:, PF_WK + 24:PF_WK + 32], func=ACTF.Copy, scale=4.0)
            ins = e.activation(out=PF[:, PF_AC + b * 16 + 8:PF_AC + b * 16 + 16],
                               in_=PF[:, PF_WK + 16:PF_WK + 24],
                               func=ACTF.Copy, scale=4.0, bias=-4.0)
            M.put(("bnac", b), s["c_a"].inc(ins))

    @blk.sync
    def _(e):
        wait_bar(e, "p15end")
        for c in range(NC2):
            if c >= 2:
                s["c_a"].wait(e, M.get(("p2sg", c - 2)))
            ins = e.dma_start(out=FB[:, (c % 2) * 2048:(c % 2) * 2048 + 2048],
                              in_=h_sp.ap()[2, c, :, :])
            M.put(("p2hl", c), s["s_h2"].inc(ins, 16))

    @blk.scalar
    def _(e):
        for c in range(NC2):
            s["s_h2"].wait(e, M.get(("p2hl", c)))
            s["c_a"].wait(e, M.get(("bnac", 2)))
            if c >= 1:
                s["c_t"].wait(e, M.get(("p2tt", c - 1)))
            for et in range(ET):
                ins = e.activation(
                    out=FB[:, 8192 + et * C2: 8192 + (et + 1) * C2],
                    in_=FB[:, (c % 2) * 2048 + et * C2:(c % 2) * 2048 + (et + 1) * C2],
                    func=ACTF.Sigmoid,
                    scale=PF[:, PF_AC + 32 + et:PF_AC + 32 + et + 1],
                    bias=PF[:, PF_AC + 40 + et:PF_AC + 40 + et + 1])
            M.put(("p2sg", c), s["c_a"].inc(ins))
            for tq in range(2):
                nt = c * 2 + tq
                for half in range(2):
                    gidx = c * 4 + tq * 2 + half
                    bank = pb[4 + gidx % 2]
                    s["c_t"].wait(e, M.get(("p2t", gidx)))
                    ins = e.activation(
                        out=B1[:, nt * 1024 + half * 512: nt * 1024 + half * 512 + 512],
                        in_=bank[:], func=ACTF.Copy)
                    M.put(("p2va", gidx), s["c_a"].inc(ins))

    @blk.tensor
    def _(e):
        s["c_v"].wait(e, M.get(("bar", "p15end", "c_v")))
        for c in range(NC2):
            s["c_a"].wait(e, M.get(("p2sg", c)))
            for tq in range(2):
                nt = c * 2 + tq
                for half in range(2):
                    gidx = c * 4 + tq * 2 + half
                    bank = pb[4 + gidx % 2]
                    if gidx >= 2:
                        pg = gidx - 2
                        s["c_a"].wait(e, M.get(("p2va", pg)))
                        s["c_v"].wait(e, M.get(("p2vv", pg)))
                    for q in range(4):
                        et = half * 4 + q
                        ins = e.matmul(bank[:, q * P:(q + 1) * P],
                                       FB[:, 8192 + et * C2 + tq * P:
                                          8192 + et * C2 + tq * P + P],
                                       eyef, is_transpose=True,
                                       start=(q == 0), stop=(q == 3))
                    M.put(("p2t", gidx), s["c_t"].inc(e.drain()))
            M.put(("p2tt", c), s["c_t"].v)
            for tq in range(2):
                nt = c * 2 + tq
                g2 = c * 4 + tq * 2
                s["c_a"].wait(e, M.get(("p2va", g2 + 1)))
                s["c_v"].wait(e, M.get(("p2vv", g2 + 1)))
                for pr in range(8):
                    bs = B1[:, nt * 1024 + pr * P: nt * 1024 + (pr + 1) * P]
                    bl = B2[:, nt * 1024 + pr * P: nt * 1024 + (pr + 1) * P]
                    first = (nt == 0 and pr % 4 == 0)
                    last = (nt == NC2 * 2 - 1 and pr % 4 == 3)
                    e.matmul(pb[pr // 4][:, (pr % 4) * P:(pr % 4 + 1) * P],
                             bs, bs, start=first, stop=last)
                    ins = e.matmul(pb[2 + pr // 4][:, (pr % 4) * P:(pr % 4 + 1) * P],
                                   bs, bl, start=first, stop=last)
            M.put(("p2gm", c), s["c_t"].inc(e.drain()))

    @blk.vector
    def _(e):
        for c in range(NC2):
            for tq in range(2):
                nt = c * 2 + tq
                for half in range(2):
                    gidx = c * 4 + tq * 2 + half
                    bank = pb[4 + gidx % 2]
                    s["c_t"].wait(e, M.get(("p2t", gidx)))
                    s["c_a"].wait(e, M.get(("p2va", gidx)))
                    ins = e.scalar_tensor_tensor(
                        out=B2[:, nt * 1024 + half * 512: nt * 1024 + half * 512 + 512],
                        in0=bank[:], scalar=0.0,
                        in1=B1[:, nt * 1024 + half * 512: nt * 1024 + half * 512 + 512],
                        op0=ALU.add, op1=ALU.subtract)
                    M.put(("p2vv", gidx), s["c_v"].inc(ins))

    @blk.scalar
    def _(e):
        s["c_t"].wait(e, M.get(("p2gm", NC2 - 1)))
        e.activation(out=FB[:, 0:512], in_=pb[0][:], func=ACTF.Copy)
        e.activation(out=FB[:, 512:1024], in_=pb[1][:], func=ACTF.Copy)
        e.activation(out=FB[:, 1024:1536], in_=pb[2][:], func=ACTF.Copy)
        ins = e.activation(out=FB[:, 1536:2048], in_=pb[3][:], func=ACTF.Copy)
        M.put("vvev", s["c_a"].inc(ins))

    @blk.tensor
    def _(e):
        s["c_a"].wait(e, M.get("vvev"))
        for half in range(2):
            bank = pb[4 + half]
            for q in range(4):
                pr = half * 4 + q
                ins = e.matmul(bank[:, q * P:(q + 1) * P],
                               FB[:, 1024 + pr * P: 1024 + (pr + 1) * P],
                               eyef, is_transpose=True, start=(q == 0), stop=(q == 3))
        M.put("vvT", s["c_t"].inc(e.drain()))

    @blk.vector
    def _(e):
        s["c_t"].wait(e, M.get("vvT"))
        e.tensor_add(out=FB[:, 2048:3072], in0=FB[:, 0:1024], in1=FB[:, 1024:2048])
        e.tensor_add(out=FB[:, 3072:3584], in0=FB[:, 2048:2560], in1=pb[4][:])
        ins = e.tensor_add(out=FB[:, 3584:4096], in0=FB[:, 2560:3072], in1=pb[5][:])
        M.put("vvsym", s["c_v"].inc(ins))

    @blk.gpsimd
    def _(e):
        s["c_v"].wait(e, M.get("vvsym"))
        s["g_ar"].inc(e.dma_start(out=ar2_i.ap(), in_=FB[:, 3072:4096]), 16)
        e.wait_ge(s["g_ar"].h, s["g_ar"].v)
        s["cc"].inc(e.collective_compute(
            "AllReduce", ALU.add, replica_groups=[list(range(n_cores))],
            ins=[ar2_i.ap()], outs=[ar2_o.ap()]), 1)
        e.wait_ge(s["cc"].h, s["cc"].v)
        M.put("ar2_done", s["g_ar"].inc(
            e.dma_start(out=FB[:, 0:1024], in_=ar2_o.ap()), 16))

    @blk.vector
    def _(e):
        s["g_ar"].wait(e, M.get("ar2_done"))
        ins = e.memset(FB[:, 1024:2048], 0.0)
        M.put("vvz", s["c_v"].inc(ins))

    @blk.scalar
    def _(e):
        s["c_v"].wait(e, M.get("vvz"))
        for pr in range(8):
            e.activation(out=FB[0:64, 1024 + pr * P: 1024 + pr * P + 64],
                         in_=FB[0:64, pr * P: pr * P + 64],
                         func=ACTF.Copy, scale=0.125)
            ins = e.activation(out=FB[64:128, 1024 + pr * P + 64: 1024 + (pr + 1) * P],
                               in_=FB[64:128, pr * P + 64: pr * P + 64 + 64],
                               func=ACTF.Copy, scale=0.125)
        M.put("vvm", s["c_a"].inc(ins))
        e.drain()
        s["c_t"].wait(e, M.get(("q2t", ET - 1, 3)))
        ins = e.activation(out=SL[:, 4096:5120], in_=FB[:, 1024:2048], func=ACTF.Copy)
        M.put("vvhi", s["c_a"].inc(ins))

    @blk.vector
    def _(e):
        s["c_a"].wait(e, M.get("vvhi"))
        ins = e.scalar_tensor_tensor(out=SL[:, 5120:6144], in0=FB[:, 1024:2048],
                                     scalar=0.0, in1=SL[:, 4096:5120],
                                     op0=ALU.add, op1=ALU.subtract)
        M.put("vvlo", s["c_v"].inc(ins))

    @blk.sync
    def _(e):
        s["c_a"].wait(e, M.get(("p2sg", NC2 - 1)))
        for dt in range(ET):
            if dt >= 2:
                s["c_t"].wait(e, M.get(("wpT", dt - 2)))
            ins = e.dma_start(out=FB[:, 4096 + (dt % 2) * 1024: 4096 + (dt % 2) * 1024 + 1024],
                              in_=w_io["p"][dt * P:(dt + 1) * P, :])
            M.put(("wpl", dt), s["s_wst"].inc(ins, 16))

    @blk.tensor
    def _(e):
        s["c_t"].wait(e, M.get(("p2gm", NC2 - 1)))
        s["c_a"].wait(e, M.get(("wba2", ET - 1, 3, 1)))
        for dt in range(ET):
            s["s_wst"].wait(e, M.get(("wpl", dt)))
            for half in range(2):
                bank = pb[6 + half]
                if dt >= 1:
                    s["c_a"].wait(e, M.get(("wpa", dt - 1, half)))
                    s["c_v"].wait(e, M.get(("wpv", dt - 1, half)))
                for q in range(4):
                    kt = half * 4 + q
                    ins = e.matmul(bank[:, q * P:(q + 1) * P],
                                   FB[:, 4096 + (dt % 2) * 1024 + kt * P:
                                      4096 + (dt % 2) * 1024 + (kt + 1) * P],
                                   eyef, is_transpose=True, start=(q == 0), stop=(q == 3))
                M.put(("wpT2", dt, half), s["c_t"].inc(e.drain()))
            M.put(("wpT", dt), s["c_t"].v)

    @blk.scalar
    def _(e):
        for dt in range(ET):
            for half in range(2):
                bank = pb[6 + half]
                s["c_t"].wait(e, M.get(("wpT2", dt, half)))
                ins = e.activation(
                    out=B1[:, 8192 + dt * 1024 + half * 512:
                           8192 + dt * 1024 + half * 512 + 512],
                    in_=bank[:], func=ACTF.Copy)
                M.put(("wpa", dt, half), s["c_a"].inc(ins))

    @blk.vector
    def _(e):
        for dt in range(ET):
            for half in range(2):
                bank = pb[6 + half]
                s["c_t"].wait(e, M.get(("wpT2", dt, half)))
                s["c_a"].wait(e, M.get(("wpa", dt, half)))
                ins = e.scalar_tensor_tensor(
                    out=B2[:, 8192 + dt * 1024 + half * 512:
                           8192 + dt * 1024 + half * 512 + 512],
                    in0=bank[:], scalar=0.0,
                    in1=B1[:, 8192 + dt * 1024 + half * 512:
                           8192 + dt * 1024 + half * 512 + 512],
                    op0=ALU.add, op1=ALU.subtract)
                M.put(("wpv", dt, half), s["c_v"].inc(ins))

    @blk.vector
    def _(e):
        bar("p2end")

    @blk.sync
    def _(e):
        wait_bar(e, "p2end")
        for c in range(NC3):
            if c >= 1:
                s["c_a"].wait(e, M.get(("p3sg", c - 1)))
            ins = e.dma_start(out=FB[:, 0:2048], in_=h_sp.ap()[0, c, :, :])
            s["s_h3"].inc(ins, 16)
            ins = e.dma_start(out=FB[:, 2048:4096], in_=h_sp.ap()[1, c, :, :])
            M.put(("p3hl", c), s["s_h3"].inc(ins, 16))
            if c >= 1:
                s["c_v"].wait(e, M.get(("p3z", c - 1)))
            ins = e.dma_start(
                out=FB[:, 4096:6144].rearrange("p (tq d) -> p tq d", tq=2),
                in_=x_io[c * C3:(c + 1) * C3, :].rearrange("(tq p) d -> p tq d", p=P))
            M.put(("p3xl", c), s["s_xr"].inc(ins, 16))

    @blk.scalar
    def _(e):
        for c in range(NC3 + 1):
            pc = c - 1
            if c >= 1:
                s["c_t"].wait(e, M.get(("p3am", pc)))
                for j in range(4):
                    ins = e.activation(
                        out=B1[:, 4096 + (pc % 2) * 2048 + j * 512:
                               4096 + (pc % 2) * 2048 + j * 512 + 512],
                        in_=pb[4 + j][:], func=ACTF.Copy)
                M.put(("p3ah", pc), s["c_a"].inc(ins))
            if c < NC3:
                s["s_h3"].wait(e, M.get(("p3hl", c)))
                if DBG and c == 2:
                    s["g_out"].wait(e, M.get("dbg_done"))
                if c >= 1:
                    s["c_v"].wait(e, M.get(("p3qk", c - 1)))
                for et in range(ET):
                    e.activation(out=FB[:, 8192 + et * C3: 8192 + (et + 1) * C3],
                                 in_=FB[:, et * C3: (et + 1) * C3],
                                 func=ACTF.Sigmoid,
                                 scale=PF[:, PF_AC + et:PF_AC + et + 1],
                                 bias=PF[:, PF_AC + 8 + et:PF_AC + 8 + et + 1])
                    ins = e.activation(out=FB[:, 10240 + et * C3: 10240 + (et + 1) * C3],
                                       in_=FB[:, 2048 + et * C3: 2048 + (et + 1) * C3],
                                       func=ACTF.Sigmoid,
                                       scale=PF[:, PF_AC + 16 + et:PF_AC + 16 + et + 1],
                                       bias=PF[:, PF_AC + 24 + et:PF_AC + 24 + et + 1])
                M.put(("p3sg", c), s["c_a"].inc(ins))
                s["c_v"].wait(e, M.get(("p3m", c)))
                ins = e.activation(out=B1[:, (c % 2) * 2048:(c % 2) * 2048 + 2048],
                                   in_=FB[:, 10240:12288], func=ACTF.Copy)
                M.put(("p3qh", c), s["c_a"].inc(ins))
            if c >= 1:
                s["c_t"].wait(e, M.get(("p3ym", pc)))
                if DBG and pc == 1:
                    s["g_out"].wait(e, M.get("dbg_done"))
                    s["g_out"].wait(e, M.get("dbg6_done"))
                for dt in range(ET):
                    bank = pb[dt // 2]
                    off = (dt % 2) * C3
                    ins = e.activation(out=FB[:, 6144 + dt * C3: 6144 + (dt + 1) * C3],
                                       in_=bank[:, off:off + C3], func=ACTF.Identity,
                                       bias=PF[:, pfv["bp"] + dt:pfv["bp"] + dt + 1])
                M.put(("p3ye", pc), s["c_a"].inc(ins))
                M.put(("p3ya", pc), s["c_a"].v)

    @blk.vector
    def _(e):
        for c in range(NC3 + 1):
            pc = c - 1
            if c >= 1:
                s["c_t"].wait(e, M.get(("p3am", pc)))
                s["c_a"].wait(e, M.get(("p3ah", pc)))
                for j in range(4):
                    ins = e.scalar_tensor_tensor(
                        out=B2[:, 4096 + (pc % 2) * 2048 + j * 512:
                               4096 + (pc % 2) * 2048 + j * 512 + 512],
                        in0=pb[4 + j][:], scalar=0.0,
                        in1=B1[:, 4096 + (pc % 2) * 2048 + j * 512:
                               4096 + (pc % 2) * 2048 + j * 512 + 512],
                        op0=ALU.add, op1=ALU.subtract)
                M.put(("p3al", pc), s["c_v"].inc(ins))
            if c < NC3:
                s["c_a"].wait(e, M.get(("p3sg", c)))
                if c >= 2:
                    s["c_t"].wait(e, M.get(("p3aw", c - 2)))
                ins = e.tensor_mul(out=FB[:, 10240:12288],
                                   in0=FB[:, 8192:10240], in1=FB[:, 10240:12288])
                M.put(("p3m", c), s["c_v"].inc(ins))
                s["c_a"].wait(e, M.get(("p3qh", c)))
                ins = e.scalar_tensor_tensor(
                    out=B2[:, (c % 2) * 2048:(c % 2) * 2048 + 2048],
                    in0=FB[:, 10240:12288], scalar=0.0,
                    in1=B1[:, (c % 2) * 2048:(c % 2) * 2048 + 2048],
                    op0=ALU.add, op1=ALU.subtract)
                M.put(("p3qk", c), s["c_v"].inc(ins))
            if c >= 1:
                s["s_xr"].wait(e, M.get(("p3xl", pc)))
                for tq in range(2):
                    tt = pc * 2 + tq
                    for half in range(2):
                        bank = pb[tq * 2 + half]
                        s["c_t"].wait(e, M.get(("p3yt", pc, tq, half)))
                        ins = e.scalar_tensor_tensor(
                            out=FA[:, tt * 1024 + half * 512: tt * 1024 + half * 512 + 512],
                            in0=bank[:], scalar=0.0,
                            in1=FB[:, 4096 + tq * 1024 + half * 512:
                                   4096 + tq * 1024 + half * 512 + 512],
                            op0=ALU.add, op1=ALU.add,
                            accum_out=PF[:, PF_ZS + tt * 2 + half: PF_ZS + tt * 2 + half + 1])
                M.put(("p3z", pc), s["c_v"].inc(ins))

    @blk.tensor
    def _(e):
        if True:
            s["c_v"].wait(e, M.get("vvlo"))
            s["c_v"].wait(e, M.get(("wpv", ET - 1, 1)))
            s["c_a"].wait(e, M.get(("wpa", ET - 1, 1)))
        for c in range(NC3):
            s["c_v"].wait(e, M.get(("p3qk", c)))
            if c >= 1:
                s["c_v"].wait(e, M.get(("p3al", c - 1)))
            for pr in range(8):
                bank = pb[4 + pr // 2]
                off = (pr % 2) * C3
                qh = B1[:, (c % 2) * 2048 + pr * C3:(c % 2) * 2048 + (pr + 1) * C3]
                ql = B2[:, (c % 2) * 2048 + pr * C3:(c % 2) * 2048 + (pr + 1) * C3]
                vh = SL[:, 4096 + pr * P: 4096 + (pr + 1) * P]
                vl = SL[:, 5120 + pr * P: 5120 + (pr + 1) * P]
                first = (pr % 2 == 0)
                e.matmul(bank[:, off:off + C3], vh, qh, start=first, stop=False)
                e.matmul(bank[:, off:off + C3], vh, ql, start=False, stop=False)
                ins = e.matmul(bank[:, off:off + C3], vl, qh, start=False,
                               stop=(pr % 2 == 1))
            M.put(("p3am", c), s["c_t"].inc(e.drain()))
            s["c_a"].wait(e, M.get(("p3ah", c)))
            s["c_v"].wait(e, M.get(("p3al", c)))
            if c >= 1:
                s["c_v"].wait(e, M.get(("p3z", c - 1)))
            for dt in range(ET):
                bank = pb[dt // 2]
                off = (dt % 2) * C3
                for ip, (wl, al) in enumerate([(0, 0), (0, 1), (1, 0)]):
                    wsrc = B1 if wl == 0 else B2
                    asrc = B1 if al == 0 else B2
                    for kt in range(ET):
                        ins = e.matmul(
                            bank[:, off:off + C3],
                            wsrc[:, 8192 + dt * 1024 + kt * P: 8192 + dt * 1024 + (kt + 1) * P],
                            asrc[:, 4096 + (c % 2) * 2048 + kt * C3:
                                 4096 + (c % 2) * 2048 + (kt + 1) * C3],
                            start=(dt % 2 == 0 and ip == 0 and kt == 0),
                            stop=(dt % 2 == 1 and ip == 2 and kt == ET - 1))
            M.put(("p3ym", c), s["c_t"].inc(e.drain()))
            M.put(("p3aw", c), s["c_t"].v)
            s["c_a"].wait(e, M.get(("p3ya", c)))
            for tq in range(2):
                for half in range(2):
                    bank = pb[tq * 2 + half]
                    for q in range(4):
                        dt = half * 4 + q
                        ins = e.matmul(bank[:, q * P:(q + 1) * P],
                                       FB[:, 6144 + dt * C3 + tq * P:
                                          6144 + dt * C3 + tq * P + P],
                                       eyef, is_transpose=True,
                                       start=(q == 0), stop=(q == 3))
                    M.put(("p3yt", c, tq, half), s["c_t"].inc(e.drain()))

    if DBG:
        @blk.gpsimd
        def _(e):
            s["c_a"].wait(e, M.get(("p0a", NG0 - 1)))
            s["c_v"].wait(e, M.get(("p0v", NG0 - 1)))
            s["g_out"].inc(e.dma_start(out=dbgx.ap()[0], in_=B1[:, 0:2048]), 16)
            s["g_out"].inc(e.dma_start(out=dbgx.ap()[1], in_=B2[:, 0:2048]), 16)
            for b in range(3):
                s["g_h"].wait(e, M.get(("p1sp", b * ET, 3)))
                s["g_out"].inc(e.dma_start(out=dbgh.ap()[b],
                                           in_=h_sp.ap()[b, 0, :, :]), 16)
            s["g_wb"].wait(e, M.get("prep_done"))
            for j in range(4):
                s["g_out"].inc(e.dma_start(out=dbgw.ap()[j],
                                           in_=wq1t_h.ap()[j, :, :]), 16)
            s["g_out"].inc(e.dma_start(
                out=dbgw.ap()[4:6].rearrange("f p c -> p f c"),
                in_=wq2t_h.ap()[0:2, :, :, :].rearrange("f p d dl -> p f (d dl)")), 16)
            s["c_v"].wait(e, M.get(("wpv", ET - 1, 1)))
            s["g_out"].inc(e.dma_start(out=dbgb.ap()[0][:, 0:1024],
                                       in_=B1[:, 11264:12288]), 16)
            s["g_out"].inc(e.dma_start(out=dbgb.ap()[1][:, 0:1024],
                                       in_=B2[:, 11264:12288]), 16)
            M.put("dbg5_done", s["g_out"].v)
            s["c_v"].wait(e, M.get(("p3qk", 0)))
            s["g_out"].inc(e.dma_start(out=dbgb.ap()[2], in_=B1[:, 0:2048]), 16)
            s["g_out"].inc(e.dma_start(out=dbgb.ap()[3], in_=B2[:, 0:2048]), 16)
            s["c_a"].wait(e, M.get(("p3ye", 0)))
            s["g_out"].inc(e.dma_start(out=dbgf.ap()[0], in_=FB[:, 6144:8192]), 16)
            M.put("dbg6_done", s["g_out"].v)
            s["c_v"].wait(e, M.get(("p3z", 0)))
            ins = e.dma_start(out=dbgf.ap()[1], in_=FA[:, 0:2048])
            M.put("dbg_done", s["g_out"].inc(ins, 16))
            s["c_v"].wait(e, M.get("p35app"))
            s["g_out"].inc(e.dma_start(out=dbgf.ap()[2], in_=FA[:, 0:2048]), 16)
            M.put("dbg2_done", s["g_out"].v)
            s["c_v"].wait(e, M.get(("p5z", 0)))
            ins = e.dma_start(out=dbgf.ap()[3], in_=FA[:, 0:2048])
            M.put("dbg3_done", s["g_out"].inc(ins, 16))
            s["c_v"].wait(e, M.get(("p5z", NC3 - 1)))
            s["g_out"].inc(e.dma_start(out=dbgf.ap()[4], in_=FA[:, (TT - 2) * 1024:]), 16)
            M.put("dbg4_done", s["g_out"].v)
            s["c_a"].wait(e, M.get("p6r"))
            s["g_out"].inc(e.dma_start(out=dbgc.ap(), in_=PF[:, :]), 16)

    @blk.vector
    def _(e):
        bar("p3end")

    @blk.scalar
    def _(e):
        wait_bar(e, "p3end")
        for tt in range(TT):
            ins = e.activation(out=FB[:, 0:1024], in_=FA[:, tt * 1024:(tt + 1) * 1024],
                               func=ACTF.Square,
                               accum_out=PF[:, PF_Z2 + tt:PF_Z2 + tt + 1])
        M.put("p35sq", s["c_a"].inc(ins))

    @blk.vector
    def _(e):
        s["c_a"].wait(e, M.get("p35sq"))
        ins = e.tensor_reduce(
            out=PF[:, PF_WK:PF_WK + 16].rearrange("p (t o) -> p t o", o=1),
            in_=PF[:, PF_ZS:PF_ZS + 32].rearrange("p (t n) -> p t n", n=2),
            axis=AXX, op=ALU.add)
        s["c_v"].inc(ins)
        e.drain()
        e.tensor_scalar(out=PF[:, PF_M:PF_M + 16], in0=PF[:, PF_WK:PF_WK + 16],
                        scalar1=1.0 / D, scalar2=None, op0=ALU.mult)
        e.tensor_scalar(out=PF[:, PF_WK:PF_WK + 16], in0=PF[:, PF_Z2:PF_Z2 + 16],
                        scalar1=1.0 / D, scalar2=None, op0=ALU.mult)
        e.drain()
        e.tensor_mul(out=PF[:, PF_WK + 16:PF_WK + 32],
                     in0=PF[:, PF_M:PF_M + 16], in1=PF[:, PF_M:PF_M + 16])
        e.drain()
        ins = e.scalar_tensor_tensor(out=PF[:, PF_WK:PF_WK + 16],
                                     in0=PF[:, PF_WK:PF_WK + 16],
                                     scalar=LN_EPS, op0=ALU.add,
                                     in1=PF[:, PF_WK + 16:PF_WK + 32],
                                     op1=ALU.subtract)
        M.put("p35v", s["c_v"].inc(ins))

    @blk.scalar
    def _(e):
        s["c_v"].wait(e, M.get("p35v"))
        ins = e.activation(out=PF[:, PF_R:PF_R + 16], in_=PF[:, PF_WK:PF_WK + 16],
                           func=ACTF.Sqrt)
        M.put("p35r", s["c_a"].inc(ins))

    @blk.vector
    def _(e):
        s["c_a"].wait(e, M.get("p35r"))
        if DBG:
            s["g_out"].wait(e, M.get("dbg_done"))
        e.reciprocal(out=PF[:, PF_R:PF_R + 16], in_=PF[:, PF_R:PF_R + 16])
        e.drain()
        for tt in range(TT):
            ins = e.tensor_scalar(out=FA[:, tt * 1024:(tt + 1) * 1024],
                                  in0=FA[:, tt * 1024:(tt + 1) * 1024],
                                  scalar1=PF[:, PF_M + tt:PF_M + tt + 1],
                                  scalar2=PF[:, PF_R + tt:PF_R + tt + 1],
                                  op0=ALU.subtract, op1=ALU.mult)
            M.put(("p35a", tt), s["c_v"].inc(ins))
        M.put("p35app", s["c_v"].v)

    @blk.tensor
    def _(e):
        for g in range(NG0):
            kt, j = divmod(g, TT // 4)
            bank = pb[g % 2]
            for q in range(4):
                tt = 4 * j + q
                s["c_v"].wait(e, M.get(("p35a", tt)))
            if g >= 2:
                s["c_a"].wait(e, M.get(("x1a", g - 2)))
                s["c_v"].wait(e, M.get(("x1v", g - 2)))
            for q in range(4):
                tt = 4 * j + q
                ins = e.matmul(bank[:, q * P:(q + 1) * P],
                               FA[:, tt * 1024 + kt * P: tt * 1024 + (kt + 1) * P],
                               eyef, is_transpose=True, start=(q == 0), stop=(q == 3))
            M.put(("x1t", g), s["c_t"].inc(e.drain()))

    @blk.scalar
    def _(e):
        for g in range(NG0):
            kt, j = divmod(g, TT // 4)
            bank = pb[g % 2]
            s["c_t"].wait(e, M.get(("x1t", g)))
            if DBG and g == 0:
                s["g_out"].wait(e, M.get("dbg5_done"))
            ins = e.activation(out=B1[:, kt * T + 4 * j * P: kt * T + (4 * j + 4) * P],
                               in_=bank[:], func=ACTF.Copy)
            M.put(("x1a", g), s["c_a"].inc(ins))

    @blk.vector
    def _(e):
        for g in range(NG0):
            kt, j = divmod(g, TT // 4)
            bank = pb[g % 2]
            s["c_t"].wait(e, M.get(("x1t", g)))
            s["c_a"].wait(e, M.get(("x1a", g)))
            ins = e.scalar_tensor_tensor(
                out=B2[:, kt * T + 4 * j * P: kt * T + (4 * j + 4) * P],
                in0=bank[:], scalar=0.0,
                in1=B1[:, kt * T + 4 * j * P: kt * T + (4 * j + 4) * P],
                op0=ALU.add, op1=ALU.subtract)
            M.put(("x1v", g), s["c_v"].inc(ins))

    @blk.vector
    def _(e):
        bar("p35end")

    @blk.sync
    def _(e):
        wait_bar(e, "p35end", extra=[("g_wb", M.get("prep_done"))])
        for c in range(NC3):
            for fc in range(NFC):
                gi = c * NFC + fc
                if gi >= 2:
                    s["c_t"].wait(e, M.get(("f1m", gi - 2)))
                    s["c_t"].wait(e, M.get(("f2done", gi - 2)))
                ins = e.dma_start(
                    out=SL[:, (fc % 2) * 2048:(fc % 2) * 2048 + 2048]
                        .rearrange("p (f col) -> p f col", f=2),
                    in_=wq1t_h.ap()[2 * fc:2 * fc + 2, :, :]
                        .rearrange("f p col -> p f col"))
                s["s_fsl"].inc(ins, 16)
                ins = e.dma_start(
                    out=SL[:, 4096 + (fc % 2) * 2048: 4096 + (fc % 2) * 2048 + 2048]
                        .rearrange("p (f col) -> p f col", f=2),
                    in_=wq2t_h.ap()[2 * fc:2 * fc + 2, :, :, :]
                        .rearrange("f p d dl -> p f (d dl)"))
                M.put(("fsl", gi), s["s_fsl"].inc(ins, 16))

    @blk.tensor
    def _(e):
        wait_bar(e, "p35end")
        for c in range(NC3):
            for fc in range(NFC + 1):
                gi = c * NFC + fc
                if fc < NFC:
                    bank = pb[4 + fc % 2]
                    s["s_fsl"].wait(e, M.get(("fsl", gi)))
                    if gi >= 2:
                        s["c_v"].wait(e, M.get(("f1r", gi - 2)))
                    if fc <= 1 and c >= 1:
                        s["c_v"].wait(e, M.get(("p5z", c - 1)))
                    for fcl in range(2):
                        for xl in range(2):
                            xs = B1 if xl == 0 else B2
                            for dt in range(ET):
                                ins = e.matmul(
                                    bank[:, fcl * C3:(fcl + 1) * C3],
                                    SL[:, (fc % 2) * 2048 + fcl * 1024 + dt * P:
                                       (fc % 2) * 2048 + fcl * 1024 + (dt + 1) * P],
                                    xs[:, dt * T + c * C3: dt * T + (c + 1) * C3],
                                    start=(fcl == 0 and xl == 0 and dt == 0),
                                    stop=(fcl == 1 and xl == 1 and dt == ET - 1))
                    M.put(("f1m", gi), s["c_t"].inc(e.drain()))
                if fc >= 1:
                    pfc = fc - 1
                    pgi = c * NFC + pfc
                    s["c_a"].wait(e, M.get(("f1h", pgi)))
                    s["c_v"].wait(e, M.get(("f1l", pgi)))
                    if fc == 1 and c >= 1:
                        s["c_a"].wait(e, M.get(("p5ye", c - 1)))
                    for dt in range(ET):
                        bank = pb[dt // 2]
                        off = (dt % 2) * C3
                        for hl in range(2):
                            hsrc_base = PB_H1H if hl == 0 else PB_H1L
                            for fcl in range(2):
                                ins = e.matmul(
                                    bank[:, off:off + C3],
                                    SL[:, 4096 + (pfc % 2) * 2048 + fcl * 1024 + dt * P:
                                       4096 + (pfc % 2) * 2048 + fcl * 1024 + (dt + 1) * P],
                                    PB[:, hsrc_base + (pfc % 2) * 512 + fcl * C3:
                                       hsrc_base + (pfc % 2) * 512 + (fcl + 1) * C3],
                                    start=(pfc == 0 and dt % 2 == 0 and hl == 0 and fcl == 0),
                                    stop=(pfc == NFC - 1 and dt % 2 == 1 and hl == 1 and fcl == 1))
                    if pfc == NFC - 1:
                        M.put(("f2done", pgi), s["c_t"].inc(e.drain()))
                        M.put(("f2last", c), s["c_t"].v)
                    else:
                        M.put(("f2done", pgi), s["c_t"].inc(ins))
            for tq in range(2):
                for half in range(2):
                    bank = pb[4 + tq * 2 + half]
                    if tq == 0 and half == 0:
                        s["c_a"].wait(e, M.get(("p5ye", c)))
                        s["c_v"].wait(e, M.get(("f1r", c * NFC + NFC - 1)))
                    for q in range(4):
                        dt = half * 4 + q
                        ins = e.matmul(bank[:, q * P:(q + 1) * P],
                                       FB[:, 1024 + dt * C3 + tq * P:
                                          1024 + dt * C3 + tq * P + P],
                                       eyef, is_transpose=True,
                                       start=(q == 0), stop=(q == 3))
                    M.put(("p5yt", c, tq, half), s["c_t"].inc(e.drain()))

    @blk.vector
    def _(e):
        for c in range(NC3):
            for fc in range(NFC):
                gi = c * NFC + fc
                bank = pb[4 + fc % 2]
                s["c_t"].wait(e, M.get(("f1m", gi)))
                if gi >= 2:
                    s["c_a"].wait(e, M.get(("f1h", gi - 2)))
                for fcl in range(2):
                    ins = e.tensor_scalar(
                        out=FB[:, (fc % 2) * 512 + fcl * C3:(fc % 2) * 512 + (fcl + 1) * C3],
                        in0=bank[:, fcl * C3:(fcl + 1) * C3],
                        scalar1=PF[:, PF_B1V + fc * 2 + fcl:PF_B1V + fc * 2 + fcl + 1],
                        scalar2=0.0, op0=ALU.add, op1=ALU.max)
                M.put(("f1r", gi), s["c_v"].inc(ins))
                s["c_a"].wait(e, M.get(("f1h", gi)))
                ins = e.scalar_tensor_tensor(
                    out=PB[:, PB_H1L + (fc % 2) * 512: PB_H1L + (fc % 2) * 512 + 512],
                    in0=FB[:, (fc % 2) * 512:(fc % 2) * 512 + 512], scalar=0.0,
                    in1=PB[:, PB_H1H + (fc % 2) * 512: PB_H1H + (fc % 2) * 512 + 512],
                    op0=ALU.add, op1=ALU.subtract)
                M.put(("f1l", gi), s["c_v"].inc(ins))
            if DBG and c == 0:
                s["g_out"].wait(e, M.get("dbg2_done"))
            for tq in range(2):
                tt = c * 2 + tq
                for half in range(2):
                    bank = pb[4 + tq * 2 + half]
                    s["c_t"].wait(e, M.get(("p5yt", c, tq, half)))
                    ins = e.scalar_tensor_tensor(
                        out=FA[:, tt * 1024 + half * 512: tt * 1024 + half * 512 + 512],
                        in0=bank[:], scalar=0.0,
                        in1=FA[:, tt * 1024 + half * 512: tt * 1024 + half * 512 + 512],
                        op0=ALU.add, op1=ALU.add,
                        accum_out=PF[:, PF_ZS + tt * 2 + half: PF_ZS + tt * 2 + half + 1])
            M.put(("p5z", c), s["c_v"].inc(ins))

    @blk.scalar
    def _(e):
        for c in range(NC3):
            for fc in range(NFC):
                gi = c * NFC + fc
                s["c_v"].wait(e, M.get(("f1r", gi)))
                if gi >= 2:
                    s["c_t"].wait(e, M.get(("f2done", gi - 2)))
                ins = e.activation(
                    out=PB[:, PB_H1H + (fc % 2) * 512: PB_H1H + (fc % 2) * 512 + 512],
                    in_=FB[:, (fc % 2) * 512:(fc % 2) * 512 + 512], func=ACTF.Copy)
                M.put(("f1h", gi), s["c_a"].inc(ins))
            s["c_t"].wait(e, M.get(("f2last", c)))
            for dt in range(ET):
                bank = pb[dt // 2]
                off = (dt % 2) * C3
                ins = e.activation(out=FB[:, 1024 + dt * C3: 1024 + (dt + 1) * C3],
                                   in_=bank[:, off:off + C3], func=ACTF.Identity,
                                   bias=PF[:, pfv["b2v"] + dt:pfv["b2v"] + dt + 1])
            M.put(("p5ye", c), s["c_a"].inc(ins))

    @blk.vector
    def _(e):
        bar("p5end")

    @blk.scalar
    def _(e):
        wait_bar(e, "p5end")
        for tt in range(TT):
            ins = e.activation(out=FB[:, 2048 + 1024 * (tt % 2): 2048 + 1024 * (tt % 2) + 1024],
                               in_=FA[:, tt * 1024:(tt + 1) * 1024],
                               func=ACTF.Square,
                               accum_out=PF[:, PF_Z2 + tt:PF_Z2 + tt + 1])
        M.put("p6sq", s["c_a"].inc(ins))

    @blk.vector
    def _(e):
        s["c_a"].wait(e, M.get("p6sq"))
        ins = e.tensor_reduce(
            out=PF[:, PF_WK:PF_WK + 16].rearrange("p (t o) -> p t o", o=1),
            in_=PF[:, PF_ZS:PF_ZS + 32].rearrange("p (t n) -> p t n", n=2),
            axis=AXX, op=ALU.add)
        s["c_v"].inc(ins)
        e.drain()
        e.tensor_scalar(out=PF[:, PF_M:PF_M + 16], in0=PF[:, PF_WK:PF_WK + 16],
                        scalar1=1.0 / D, scalar2=None, op0=ALU.mult)
        e.tensor_scalar(out=PF[:, PF_WK:PF_WK + 16], in0=PF[:, PF_Z2:PF_Z2 + 16],
                        scalar1=1.0 / D, scalar2=None, op0=ALU.mult)
        e.drain()
        e.tensor_mul(out=PF[:, PF_WK + 16:PF_WK + 32],
                     in0=PF[:, PF_M:PF_M + 16], in1=PF[:, PF_M:PF_M + 16])
        e.drain()
        ins = e.scalar_tensor_tensor(out=PF[:, PF_WK:PF_WK + 16],
                                     in0=PF[:, PF_WK:PF_WK + 16],
                                     scalar=LN_EPS, op0=ALU.add,
                                     in1=PF[:, PF_WK + 16:PF_WK + 32],
                                     op1=ALU.subtract)
        M.put("p6v", s["c_v"].inc(ins))

    @blk.scalar
    def _(e):
        s["c_v"].wait(e, M.get("p6v"))
        ins = e.activation(out=PF[:, PF_R:PF_R + 16], in_=PF[:, PF_WK:PF_WK + 16],
                           func=ACTF.Sqrt)
        M.put("p6r", s["c_a"].inc(ins))

    @blk.vector
    def _(e):
        s["c_a"].wait(e, M.get("p6r"))
        if DBG:
            s["g_out"].wait(e, M.get("dbg3_done"))
            s["g_out"].wait(e, M.get("dbg4_done"))
        e.reciprocal(out=PF[:, PF_R:PF_R + 16], in_=PF[:, PF_R:PF_R + 16])
        e.drain()
        for tt in range(TT):
            ins = e.tensor_scalar(out=FA[:, tt * 1024:(tt + 1) * 1024],
                                  in0=FA[:, tt * 1024:(tt + 1) * 1024],
                                  scalar1=PF[:, PF_M + tt:PF_M + tt + 1],
                                  scalar2=PF[:, PF_R + tt:PF_R + tt + 1],
                                  op0=ALU.subtract, op1=ALU.mult)
            M.put(("p6a", tt), s["c_v"].inc(ins))

    @blk.gpsimd
    def _(e):
        for tt in range(TT):
            s["c_v"].wait(e, M.get(("p6a", tt)))
            ins = e.dma_start(out=out_io[tt * P:(tt + 1) * P, :],
                              in_=FA[:, tt * 1024:(tt + 1) * 1024])
            s["g_out"].inc(ins, 16)
        e.wait_ge(s["g_out"].h, s["g_out"].v)

    stack.close()
    return nc


_BUILD_CACHE = {}


def _get_nc(n_cores):
    if n_cores not in _BUILD_CACHE:
        marks = Marks()
        build_nc(n_cores, marks)
        marks.check = True
        _BUILD_CACHE[n_cores] = build_nc(n_cores, marks)
    return _BUILD_CACHE[n_cores]


def make_inputs(L0_attn_x, L0_attn_Wq, L0_attn_gq, L0_attn_bq, L0_attn_Wk,
                L0_attn_gk, L0_attn_bk, L0_attn_Wv, L0_attn_gv, L0_attn_bv,
                L0_attn_Wp, L0_attn_bp, L0_ffn_W1, L0_ffn_b1, L0_ffn_W2,
                L0_ffn_b2, n_cores):
    f32 = lambda a: np.ascontiguousarray(np.asarray(a, dtype=np.float32))
    vec = lambda v, r: np.ascontiguousarray(
        np.asarray(v, dtype=np.float32).reshape(r, P).T)
    x = f32(L0_attn_x).reshape(-1, D)
    shards = np.split(x, n_cores, axis=0)
    common = {
        "wq": f32(L0_attn_Wq), "wk": f32(L0_attn_Wk), "wv": f32(L0_attn_Wv),
        "wp": f32(L0_attn_Wp),
        "w1": f32(L0_ffn_W1), "w2": f32(L0_ffn_W2),
        "gq": vec(L0_attn_gq, 8), "bq": vec(L0_attn_bq, 8),
        "gk": vec(L0_attn_gk, 8), "bk": vec(L0_attn_bk, 8),
        "gv": vec(L0_attn_gv, 8), "bv": vec(L0_attn_bv, 8),
        "bp": vec(L0_attn_bp, 8), "b2v": vec(L0_ffn_b2, 8),
        "b1v": vec(L0_ffn_b1, 32),
        "eyef": np.eye(P, dtype=np.float32),
        "onesq": np.ones((P, P), dtype=np.float32),
        "eyeb": np.eye(P, dtype=ml_dtypes.bfloat16),
        "ones128": np.ones((P, 1), dtype=np.float32),
    }
    return [dict(common, x=np.ascontiguousarray(sh)) for sh in shards]


def kernel(**inputs):
    n_cores = N_CORES
    in_maps = make_inputs(n_cores=n_cores, **inputs)
    nc = _get_nc(n_cores)
    res = run_bass_kernel_spmd(nc, in_maps, core_ids=list(range(n_cores)))
    out = np.concatenate([res.results[c]["out"] for c in range(n_cores)], axis=0)
    return out.reshape(1, n_cores * T, D)


# version 09 (final)
# speedup vs baseline: 1.2162x; 1.2162x over previous
import contextlib
import numpy as np
import ml_dtypes
import concourse.bass as bass
import concourse.mybir as mybir
from concourse.bass_utils import run_bass_kernel_spmd

F32 = mybir.dt.float32
BF16 = mybir.dt.bfloat16
ALU = mybir.AluOpType
ACTF = mybir.ActivationFunctionType
AXX = mybir.AxisListType.X

P = 128
D = 1024
ET = 8
DFF = 4096
FT = 32
T = 2048
TT = T // P
C2 = 256
NC2 = T // C2
C3 = 256
NC3 = T // C3
FC = 256
NFC = DFF // FC
BN_EPS = 1e-5
LN_EPS = 1e-5
FFN_EPS = 1e-5

W_ORD = ("q", "k", "v")
N_CORES = 8
DBG = False


class Marks:

    def __init__(self):
        self.d = {}
        self.check = False

    def put(self, key, val):
        if self.check and key in self.d and self.d[key] != val:
            raise AssertionError(f"mark drift {key}: pass1={self.d[key]} pass2={val}")
        self.d[key] = val
        return val

    def get(self, key):
        return self.d.get(key, 0)


class Sem:
    def __init__(self, stack, nc, name):
        self.h = stack.enter_context(nc.semaphore(name))
        self.v = 0

    def inc(self, instr, n=1):
        instr.then_inc(self.h, n)
        self.v += n
        return self.v

    def wait(self, eng, val):
        if val > 0:
            eng.wait_ge(self.h, val)


def build_nc(n_cores, marks):
    NTOT = T * n_cores
    M = marks

    nc = bass.Bass("TRN2", target_bir_lowering=False, debug=False,
                   detect_race_conditions=False)

    x_io = nc.dram_tensor("x", [T, D], F32, kind="ExternalInput").ap()
    w_io = {b: nc.dram_tensor(f"w{b}", [D, D], F32, kind="ExternalInput").ap()
            for b in ("q", "k", "v", "p")}
    w1_io = nc.dram_tensor("w1", [DFF, D], F32, kind="ExternalInput").ap()
    w2_io = nc.dram_tensor("w2", [D, DFF], F32, kind="ExternalInput").ap()
    vec_io = {nm: nc.dram_tensor(nm, [P, 8], F32, kind="ExternalInput").ap()
              for nm in ("gq", "bq", "gk", "bk", "gv", "bv", "bp", "b2v")}
    b1v_io = nc.dram_tensor("b1v", [P, 32], F32, kind="ExternalInput").ap()
    eyef_io = nc.dram_tensor("eyef", [P, P], F32, kind="ExternalInput").ap()
    onesq_io = nc.dram_tensor("onesq", [P, P], F32, kind="ExternalInput").ap()
    eyeb_io = nc.dram_tensor("eyeb", [P, P], BF16, kind="ExternalInput").ap()
    ones_io = nc.dram_tensor("ones128", [P, 1], F32, kind="ExternalInput").ap()
    out_io = nc.dram_tensor("out", [T, D], F32, kind="ExternalOutput").ap()

    h_sp = nc.dram_tensor("h_sp", [3, NC3, P, ET * C3], F32)
    wqall_h = nc.dram_tensor("wqall", [NFC, P, 4096], BF16)
    ar1_i = nc.dram_tensor("ar1_i", [P, 48], F32)
    ar1_o = nc.dram_tensor("ar1_o", [P, 48], F32)
    ar2_i = nc.dram_tensor("ar2_i", [P, 1024], F32)
    ar2_o = nc.dram_tensor("ar2_o", [P, 1024], F32)
    if DBG:
        dbgb = nc.dram_tensor("dbgb", [4, P, 2048], BF16, kind="ExternalOutput")
        dbgf = nc.dram_tensor("dbgf", [5, P, 2048], F32, kind="ExternalOutput")
        dbgx = nc.dram_tensor("dbgx", [2, P, 2048], BF16, kind="ExternalOutput")
        dbgh = nc.dram_tensor("dbgh", [3, P, 2048], F32, kind="ExternalOutput")
        dbgc = nc.dram_tensor("dbgc", [P, 640], F32, kind="ExternalOutput")
        dbgw = nc.dram_tensor("dbgw", [6, P, 1024], BF16, kind="ExternalOutput")

    stack = contextlib.ExitStack()
    sb = lambda name, shape, dt: stack.enter_context(nc.sbuf_tensor(name, shape, dt))
    psb = lambda name, shape: stack.enter_context(nc.psum_tensor(name, shape, F32))

    FA = sb("FA", [P, TT * 1024], F32)
    FB = sb("FB", [P, 12288], F32)
    B1 = sb("B1", [P, TT * 1024], BF16)
    B2 = sb("B2", [P, TT * 1024], BF16)
    SL = sb("SL", [P, 12288], BF16)
    PF = sb("PF", [P, 640], F32)
    ONESQ = sb("ONESQ", [P, P], F32)
    PB = sb("PB", [P, 2304], BF16)
    pb = [psb(f"pb{i}", [P, 512]) for i in range(8)]

    pfv = {"gq": 0, "bq": 8, "gk": 16, "bk": 24, "gv": 32, "bv": 40,
           "bp": 48, "b2v": 56}
    PF_B1V = 64
    PF_EYE = 96
    PF_ONE = 224
    PF_ST = 232
    PF_SR = 280
    PF_AC = 328
    PF_WK = 376
    PF_ZS = 408
    PF_Z2 = 440
    PF_M = 456
    PF_R = 472
    PF_G = 488
    PF_ACb = 504
    PB_EYEB = 0
    PB_ST = 128
    PB_H1H = 128
    PB_H1L = 1152

    sems = {}
    for nm in ("s_x", "s_vec", "s_wst", "s_h2", "s_h3", "s_xr", "s_fsl",
               "s_wg", "g_h", "g_wb", "g_ar", "g_out",
               "cc", "c_t", "c_a", "c_v", "c_g"):
        sems[nm] = Sem(stack, nc, nm)
    s = sems

    blk = stack.enter_context(nc.Block())

    eyef = PF[:, PF_EYE:PF_EYE + P]
    eyeb = PB[:, PB_EYEB:PB_EYEB + P]
    ones128 = PF[:, PF_ONE:PF_ONE + 1]

    def bar(name):
        for nm in ("c_t", "c_a", "c_v", "c_g"):
            M.put(("bar", name, nm), s[nm].v)

    def wait_bar(e, name, extra=()):
        for nm in ("c_t", "c_a", "c_v", "c_g"):
            s[nm].wait(e, M.get(("bar", name, nm)))
        for nm, val in extra:
            s[nm].wait(e, val)

    @blk.sync
    def _(e):
        M.put("x_ld", s["s_x"].inc(
            e.dma_start(out=FA[:, :].rearrange("p (tt d) -> p tt d", tt=TT),
                        in_=x_io.rearrange("(tt p) d -> p tt d", p=P)), 16))
        for nm, col in pfv.items():
            s["s_vec"].inc(e.dma_start(out=PF[:, col:col + 8], in_=vec_io[nm]), 16)
        s["s_vec"].inc(e.dma_start(out=PF[:, PF_B1V:PF_B1V + 32], in_=b1v_io), 16)
        s["s_vec"].inc(e.dma_start(out=PF[:, PF_EYE:PF_EYE + P], in_=eyef_io), 16)
        s["s_vec"].inc(e.dma_start(out=PF[:, PF_ONE:PF_ONE + 1], in_=ones_io), 16)
        s["s_vec"].inc(e.dma_start(out=ONESQ[:, :], in_=onesq_io), 16)
        M.put("vec_ld", s["s_vec"].inc(
            e.dma_start(out=PB[:, PB_EYEB:PB_EYEB + P], in_=eyeb_io), 16))

    NG0 = ET * (TT // 4)

    @blk.tensor
    def _(e):
        s["s_x"].wait(e, M.get("x_ld"))
        s["s_vec"].wait(e, M.get("vec_ld"))
        for g in range(NG0):
            kt, j = divmod(g, TT // 4)
            bank = pb[g % 2]
            if g >= 2:
                s["c_a"].wait(e, M.get(("p0a", g - 2)))
                s["c_v"].wait(e, M.get(("p0v", g - 2)))
            for q in range(4):
                tt = 4 * j + q
                ins = e.matmul(bank[:, q * P:(q + 1) * P],
                               FA[:, tt * 1024 + kt * P: tt * 1024 + (kt + 1) * P],
                               eyef, is_transpose=True, start=(q == 0), stop=(q == 3))
            M.put(("p0t", g), s["c_t"].inc(e.drain()))

    @blk.scalar
    def _(e):
        for g in range(NG0):
            kt, j = divmod(g, TT // 4)
            bank = pb[g % 2]
            s["c_t"].wait(e, M.get(("p0t", g)))
            ins = e.activation(out=B1[:, kt * T + 4 * j * P: kt * T + (4 * j + 4) * P],
                               in_=bank[:], func=ACTF.Copy)
            M.put(("p0a", g), s["c_a"].inc(ins))

    @blk.vector
    def _(e):
        for g in range(NG0):
            kt, j = divmod(g, TT // 4)
            bank = pb[g % 2]
            s["c_t"].wait(e, M.get(("p0t", g)))
            s["c_a"].wait(e, M.get(("p0a", g)))
            ins = e.scalar_tensor_tensor(
                out=B2[:, kt * T + 4 * j * P: kt * T + (4 * j + 4) * P],
                in0=bank[:], scalar=0.0,
                in1=B1[:, kt * T + 4 * j * P: kt * T + (4 * j + 4) * P],
                op0=ALU.add, op1=ALU.subtract)
            M.put(("p0v", g), s["c_v"].inc(ins))

    NSL = 3 * ET

    @blk.sync
    def _(e):
        for i in range(NSL):
            b, et = divmod(i, ET)
            if i >= 2:
                s["c_t"].wait(e, M.get(("p1wt", i - 2)))
            ins = e.dma_start(out=FB[:, (i % 2) * 1024:(i % 2) * 1024 + 1024],
                              in_=w_io[W_ORD[b]][et * P:(et + 1) * P, :])
            M.put(("p1wl", i), s["s_wst"].inc(ins, 16))

    @blk.tensor
    def _(e):
        for i in range(NSL + 1):
            if i < NSL:
                sl = i % 2
                s["s_wst"].wait(e, M.get(("p1wl", i)))
                for half in range(2):
                    bank = pb[4 + half]
                    if i >= 1:
                        s["c_a"].wait(e, M.get(("p1wa", i - 1, half)))
                        s["c_v"].wait(e, M.get(("p1wv", i - 1, half)))
                    for q in range(4):
                        kt = half * 4 + q
                        ins = e.matmul(bank[:, q * P:(q + 1) * P],
                                       FB[:, sl * 1024 + kt * P: sl * 1024 + (kt + 1) * P],
                                       eyef, is_transpose=True, start=(q == 0), stop=(q == 3))
                    M.put(("p1wt2", i, half), s["c_t"].inc(e.drain()))
                M.put(("p1wt", i), s["c_t"].v)
            if i >= 1:
                j = i - 1
                sl = j % 2
                s["c_a"].wait(e, M.get(("p1wa", j, 1)))
                s["c_v"].wait(e, M.get(("p1wv", j, 1)))
                passes = ([(0, kt, 0) for kt in range(ET)]
                          + [(1, kt, 0) for kt in range(ET)]
                          + [(0, kt, 1) for kt in range(ET)])
                for h in range(2):
                    if j == 0 and h == 0:
                        s["c_a"].wait(e, M.get(("p0a", NG0 - 1)))
                        s["c_v"].wait(e, M.get(("p0v", NG0 - 1)))
                    if j >= 1:
                        s["c_a"].wait(e, M.get(("p1ha", j - 1, h * 2 + 1)))
                        s["c_v"].wait(e, M.get(("p1hv", j - 1, h * 2 + 1)))
                    for ip, (wl, kt, xl) in enumerate(passes):
                        wt = SL[:, 2048 * wl + sl * 1024: 2048 * wl + (sl + 1) * 1024]
                        xs = B1 if xl == 0 else B2
                        for nt in (h * 2, h * 2 + 1):
                            ins = e.matmul(pb[nt][:],
                                           wt[:, kt * P:(kt + 1) * P],
                                           xs[:, kt * T + nt * 512: kt * T + (nt + 1) * 512],
                                           start=(ip == 0), stop=(ip == len(passes) - 1))
                    M.put(("p1mmh", j, h), s["c_t"].inc(e.drain()))
                M.put(("p1mm", j), s["c_t"].v)

    @blk.scalar
    def _(e):
        for i in range(NSL):
            b, et = divmod(i, ET)
            sl = i % 2
            for half in range(2):
                bank = pb[4 + half]
                s["c_t"].wait(e, M.get(("p1wt2", i, half)))
                ins = e.activation(
                    out=SL[:, sl * 1024 + half * 512: sl * 1024 + half * 512 + 512],
                    in_=bank[:], func=ACTF.Copy)
                M.put(("p1wa", i, half), s["c_a"].inc(ins))
            for nt in range(4):
                s["c_t"].wait(e, M.get(("p1mmh", i, nt // 2)))
                if i >= 1:
                    s["g_h"].wait(e, M.get(("p1sp", i - 1, nt)))
                ins = e.activation(
                    out=FB[:, 2048 + nt * 512: 2048 + nt * 512 + 512],
                    in_=pb[nt][:], func=ACTF.Copy,
                    accum_out=PF[:, PF_ACb + et * 4 + nt: PF_ACb + et * 4 + nt + 1])
                M.put(("p1ha", i, nt), s["c_a"].inc(ins))
            M.put(("p1ea", i), s["c_a"].v)

    @blk.vector
    def _(e):
        for i in range(NSL):
            b, et = divmod(i, ET)
            sl = i % 2
            for half in range(2):
                bank = pb[4 + half]
                s["c_t"].wait(e, M.get(("p1wt2", i, half)))
                s["c_a"].wait(e, M.get(("p1wa", i, half)))
                ins = e.scalar_tensor_tensor(
                    out=SL[:, 2048 + sl * 1024 + half * 512: 2048 + sl * 1024 + half * 512 + 512],
                    in0=bank[:], scalar=0.0,
                    in1=SL[:, sl * 1024 + half * 512: sl * 1024 + half * 512 + 512],
                    op0=ALU.add, op1=ALU.subtract)
                M.put(("p1wv", i, half), s["c_v"].inc(ins))
            for nt in range(4):
                s["c_a"].wait(e, M.get(("p1ha", i, nt)))
                stg = FB[:, 2048 + nt * 512: 2048 + nt * 512 + 512]
                ins = e.scalar_tensor_tensor(
                    out=FB[:, 4096:4608], in0=stg, scalar=0.0,
                    in1=stg, op0=ALU.add, op1=ALU.mult,
                    accum_out=PF[:, PF_ACb + 32 + et * 4 + nt: PF_ACb + 32 + et * 4 + nt + 1])
                M.put(("p1hv", i, nt), s["c_v"].inc(ins))
            M.put(("p1ev", i), s["c_v"].v)
            if et == ET - 1:
                s["c_a"].wait(e, M.get(("p1ea", i)))
                e.drain()
                ins = e.tensor_reduce(
                    out=PF[:, PF_ST + b * 16: PF_ST + b * 16 + 8]
                        .rearrange("p (e o) -> p e o", o=1),
                    in_=PF[:, PF_ACb:PF_ACb + 32].rearrange("p (e n) -> p e n", n=4),
                    axis=AXX, op=ALU.add)
                s["c_v"].inc(ins)
                ins = e.tensor_reduce(
                    out=PF[:, PF_ST + b * 16 + 8: PF_ST + b * 16 + 16]
                        .rearrange("p (e o) -> p e o", o=1),
                    in_=PF[:, PF_ACb + 32:PF_ACb + 64].rearrange("p (e n) -> p e n", n=4),
                    axis=AXX, op=ALU.add)
                M.put(("p1st", b), s["c_v"].inc(ins))

    @blk.gpsimd
    def _(e):
        for i in range(NSL):
            b, et = divmod(i, ET)
            for nt in range(4):
                s["c_a"].wait(e, M.get(("p1ha", i, nt)))
                s["c_v"].wait(e, M.get(("p1hv", i, nt)))
                ins = e.dma_start(
                    out=h_sp.ap()[b, 2 * nt:2 * nt + 2, :, et * C3:(et + 1) * C3]
                        .rearrange("c p n -> p c n"),
                    in_=FB[:, 2048 + nt * 512: 2048 + nt * 512 + 512]
                        .rearrange("p (c n) -> p c n", c=2))
                M.put(("p1sp", i, nt), s["g_h"].inc(ins, 16))

    @blk.gpsimd
    def _(e):
        s["c_v"].wait(e, M.get(("p1st", 2)))
        s["g_ar"].inc(e.dma_start(out=ar1_i.ap(), in_=PF[:, PF_ST:PF_ST + 48]), 16)
        e.wait_ge(s["g_ar"].h, s["g_ar"].v)
        s["cc"].inc(e.collective_compute(
            "AllReduce", ALU.add, replica_groups=[list(range(n_cores))],
            ins=[ar1_i.ap()], outs=[ar1_o.ap()]), 1)
        e.wait_ge(s["cc"].h, s["cc"].v)
        M.put("ar1_done", s["g_ar"].inc(
            e.dma_start(out=PF[:, PF_SR:PF_SR + 48], in_=ar1_o.ap()), 16))

    @blk.tensor
    def _(e):
        bar("p1end")

    @blk.sync
    def _(e):
        wait_bar(e, "p1end", extra=[("g_h", M.get(("p1sp", NSL - 1, 3)))])
        for c in range(FT):
            if c >= 2:
                s["c_v"].wait(e, M.get(("g1r", c - 2)))
            ins = e.dma_start(out=FB[:, (c % 2) * 1024:(c % 2) * 1024 + 1024],
                              in_=w1_io[c * P:(c + 1) * P, :])
            M.put(("g1l", c), s["s_wg"].inc(ins, 16))
        for c in range(ET):
            if c >= 2:
                s["c_v"].wait(e, M.get(("g2r", c - 2)))
            ins = e.dma_start(out=FB[:, 2048 + (c % 2) * 4096: 2048 + (c % 2) * 4096 + 4096],
                              in_=w2_io[c * P:(c + 1) * P, :])
            M.put(("g2l", c), s["s_wg"].inc(ins, 16))

    @blk.vector
    def _(e):
        for c in range(FT):
            s["s_wg"].wait(e, M.get(("g1l", c)))
            ins = e.tensor_reduce(out=PF[:, PF_ACb + c:PF_ACb + c + 1],
                                  in_=FB[:, (c % 2) * 1024:(c % 2) * 1024 + 1024],
                                  axis=AXX, op=ALU.add, apply_absolute_value=True)
            M.put(("g1r", c), s["c_v"].inc(ins))
        for c in range(ET):
            s["s_wg"].wait(e, M.get(("g2l", c)))
            ins = e.tensor_reduce(out=PF[:, PF_ACb + 32 + c:PF_ACb + 33 + c],
                                  in_=FB[:, 2048 + (c % 2) * 4096: 2048 + (c % 2) * 4096 + 4096],
                                  axis=AXX, op=ALU.add, apply_absolute_value=True)
            M.put(("g2r", c), s["c_v"].inc(ins))
        e.drain()
        ins = e.tensor_reduce(out=PF[:, PF_G:PF_G + 1],
                              in_=PF[:, PF_ACb:PF_ACb + 32]
                                  .rearrange("p (o n) -> p o n", o=1),
                              axis=AXX, op=ALU.add)
        s["c_v"].inc(ins)
        ins = e.tensor_reduce(out=PF[:, PF_G + 1:PF_G + 2],
                              in_=PF[:, PF_ACb + 32:PF_ACb + 40]
                                  .rearrange("p (o n) -> p o n", o=1),
                              axis=AXX, op=ALU.add)
        M.put("gsum", s["c_v"].inc(ins))

    @blk.tensor
    def _(e):
        s["c_v"].wait(e, M.get("gsum"))
        ins = e.matmul(pb[6][:, 0:2], ONESQ[:, :], PF[:, PF_G:PF_G + 2],
                       start=True, stop=True)
        M.put("gtot", s["c_t"].inc(e.drain()))

    @blk.scalar
    def _(e):
        s["c_t"].wait(e, M.get("gtot"))
        ins = e.activation(out=PF[:, PF_G + 4:PF_G + 6], in_=pb[6][:, 0:2],
                           func=ACTF.Copy, scale=0.5 / (DFF * D), bias=0.5 * FFN_EPS)
        M.put("gthr", s["c_a"].inc(ins))

    @blk.vector
    def _(e):
        s["c_a"].wait(e, M.get("gthr"))
        ins = e.tensor_scalar(out=PF[:, PF_G + 6:PF_G + 8],
                              in0=PF[:, PF_G + 4:PF_G + 6],
                              scalar1=-1.0, scalar2=None, op0=ALU.mult)
        M.put("gneg", s["c_v"].inc(ins))
        e.drain()

    @blk.sync
    def _(e):
        s["c_v"].wait(e, M.get("gneg"))
        for c in range(FT):
            if c >= 2:
                s["c_v"].wait(e, M.get(("q1q", c - 2)))
            ins = e.dma_start(out=FB[:, (c % 2) * 1024:(c % 2) * 1024 + 1024],
                              in_=w1_io[c * P:(c + 1) * P, :])
            M.put(("q1l", c), s["s_wg"].inc(ins, 16))
        for c in range(ET):
            if c >= 2:
                s["c_v"].wait(e, M.get(("q2q", c - 2, 3)))
            ins = e.dma_start(out=FB[:, 2048 + (c % 2) * 4096: 2048 + (c % 2) * 4096 + 4096],
                              in_=w2_io[c * P:(c + 1) * P, :])
            M.put(("q2l", c), s["s_wg"].inc(ins, 16))

    @blk.vector
    def _(e):
        for c in range(FT):
            s["s_wg"].wait(e, M.get(("q1l", c)))
            src = FB[:, (c % 2) * 1024:(c % 2) * 1024 + 1024]
            ins = e.tensor_scalar(out=FB[:, 10240:11264], in0=src,
                                  scalar1=PF[:, PF_G + 6:PF_G + 7], scalar2=None,
                                  op0=ALU.is_lt)
            s["c_v"].inc(ins)
            if c >= 1:
                s["c_t"].wait(e, M.get(("q1t", c - 1)))
            ins = e.scalar_tensor_tensor(
                out=FB[:, 11264:12288],
                in0=src, scalar=PF[:, PF_G + 4:PF_G + 5], op0=ALU.is_gt,
                in1=FB[:, 10240:11264], op1=ALU.subtract)
            M.put(("q1q", c), s["c_v"].inc(ins))
        for c in range(ET):
            s["s_wg"].wait(e, M.get(("q2l", c)))
            src = FB[:, 2048 + (c % 2) * 4096: 2048 + (c % 2) * 4096 + 4096]
            for hh in range(4):
                ins = e.tensor_scalar(out=FB[:, 10240:11264],
                                      in0=src[:, hh * 1024:(hh + 1) * 1024],
                                      scalar1=PF[:, PF_G + 7:PF_G + 8], scalar2=None,
                                      op0=ALU.is_lt)
                s["c_v"].inc(ins)
                g = c * 4 + hh
                if g >= 1:
                    s["c_t"].wait(e, M.get(("q2t", (g - 1) // 4, (g - 1) % 4)))
                else:
                    s["c_t"].wait(e, M.get(("q1t", FT - 1)))
                ins = e.scalar_tensor_tensor(
                    out=FB[:, 11264:12288],
                    in0=src[:, hh * 1024:(hh + 1) * 1024],
                    scalar=PF[:, PF_G + 5:PF_G + 6], op0=ALU.is_gt,
                    in1=FB[:, 10240:11264], op1=ALU.subtract)
                M.put(("q2q", c, hh), s["c_v"].inc(ins))

    @blk.tensor
    def _(e):
        for c in range(FT):
            s["c_v"].wait(e, M.get(("q1q", c)))
            for half in range(2):
                bank = pb[6 + half]
                if c == 0:
                    s["c_a"].wait(e, M.get("gthr"))
                if c >= 1:
                    s["c_a"].wait(e, M.get(("wba1", c - 1, half)))
                for q in range(4):
                    dt = half * 4 + q
                    ins = e.matmul(bank[:, q * P:(q + 1) * P],
                                   FB[:, 11264 + dt * P: 11264 + (dt + 1) * P],
                                   eyef, is_transpose=True, start=(q == 0), stop=(q == 3))
                M.put(("q1t2", c, half), s["c_t"].inc(e.drain()))
            M.put(("q1t", c), s["c_t"].v)
        for c in range(ET):
            for hh in range(4):
                g = c * 4 + hh
                s["c_v"].wait(e, M.get(("q2q", c, hh)))
                for half in range(2):
                    bank = pb[6 + half]
                    if g >= 1:
                        s["c_a"].wait(e, M.get(("wba2", (g - 1) // 4, (g - 1) % 4, half)))
                    else:
                        s["c_a"].wait(e, M.get(("wba1", FT - 1, half)))
                    for q in range(4):
                        ft = half * 4 + q
                        ins = e.matmul(bank[:, q * P:(q + 1) * P],
                                       FB[:, 11264 + ft * P: 11264 + (ft + 1) * P],
                                       eyef, is_transpose=True,
                                       start=(q == 0), stop=(q == 3))
                    M.put(("q2t2", c, hh, half), s["c_t"].inc(e.drain()))
                M.put(("q2t", c, hh), s["c_t"].v)

    @blk.scalar
    def _(e):
        for c in range(FT):
            for half in range(2):
                s["c_t"].wait(e, M.get(("q1t2", c, half)))
                if c >= 2:
                    s["g_wb"].wait(e, M.get(("wb1", c - 2)))
                ins = e.activation(
                    out=PB[:, 128 + (c % 2) * 1024 + half * 512:
                           128 + (c % 2) * 1024 + half * 512 + 512],
                    in_=pb[6 + half][:], func=ACTF.Copy)
                M.put(("wba1", c, half), s["c_a"].inc(ins))
        for c in range(ET):
            for hh in range(4):
                g = c * 4 + hh
                for half in range(2):
                    s["c_t"].wait(e, M.get(("q2t2", c, hh, half)))
                    if g >= 2:
                        s["g_wb"].wait(e, M.get(("wb2", (g - 2) // 4, (g - 2) % 4)))
                    ins = e.activation(
                        out=PB[:, 128 + (g % 2) * 1024 + half * 512:
                               128 + (g % 2) * 1024 + half * 512 + 512],
                        in_=pb[6 + half][:], func=ACTF.Copy)
                    M.put(("wba2", c, hh, half), s["c_a"].inc(ins))

    @blk.gpsimd
    def _(e):
        for c in range(FT):
            s["c_a"].wait(e, M.get(("wba1", c, 1)))
            ins = e.dma_start(
                out=wqall_h.ap()[c // 2, :, (c % 2) * 1024:(c % 2) * 1024 + 1024],
                in_=PB[:, 128 + (c % 2) * 1024: 128 + (c % 2) * 1024 + 1024])
            M.put(("wb1", c), s["g_wb"].inc(ins, 16))
        for c in range(ET):
            for hh in range(4):
                g = c * 4 + hh
                s["c_a"].wait(e, M.get(("wba2", c, hh, 1)))
                stg = PB[:, 128 + (g % 2) * 1024: 128 + (g % 2) * 1024 + 1024] \
                    .rearrange("p (f s dl) -> p f s dl", f=4, s=2)
                ins = e.dma_start(
                    out=wqall_h.ap()[hh * 4:(hh + 1) * 4, :, 2048 + c * P: 2048 + (c + 1) * P]
                        .rearrange("f p dl -> p f dl"),
                    in_=stg[:, :, 0, :])
                s["g_wb"].inc(ins, 16)
                ins = e.dma_start(
                    out=wqall_h.ap()[hh * 4:(hh + 1) * 4, :, 3072 + c * P: 3072 + (c + 1) * P]
                        .rearrange("f p dl -> p f dl"),
                    in_=stg[:, :, 1, :])
                M.put(("wb2", c, hh), s["g_wb"].inc(ins, 16))
        M.put("prep_done", s["g_wb"].v)

    @blk.vector
    def _(e):
        bar("p15end")

    @blk.vector
    def _(e):
        s["g_ar"].wait(e, M.get("ar1_done"))
        for b in range(3):
            if b >= 1:
                s["c_a"].wait(e, M.get(("bnac", b - 1)))
            sr = PF[:, PF_SR + b * 16:PF_SR + b * 16 + 8]
            qr = PF[:, PF_SR + b * 16 + 8:PF_SR + b * 16 + 16]
            g_v = PF[:, pfv[("gq", "gk", "gv")[b]]:pfv[("gq", "gk", "gv")[b]] + 8]
            e.tensor_scalar(out=PF[:, PF_WK:PF_WK + 8], in0=sr,
                            scalar1=1.0 / NTOT, scalar2=None, op0=ALU.mult)
            e.tensor_scalar(out=PF[:, PF_WK + 8:PF_WK + 16], in0=qr,
                            scalar1=1.0 / NTOT, scalar2=None, op0=ALU.mult)
            e.drain()
            e.tensor_mul(out=PF[:, PF_WK + 16:PF_WK + 24],
                         in0=PF[:, PF_WK:PF_WK + 8], in1=PF[:, PF_WK:PF_WK + 8])
            e.drain()
            ins = e.scalar_tensor_tensor(out=PF[:, PF_WK + 16:PF_WK + 24],
                                         in0=PF[:, PF_WK + 8:PF_WK + 16],
                                         scalar=BN_EPS, op0=ALU.add,
                                         in1=PF[:, PF_WK + 16:PF_WK + 24],
                                         op1=ALU.subtract)
            M.put(("bnv", b), s["c_v"].inc(ins))
            s["c_a"].wait(e, M.get(("bnr", b)))
            e.reciprocal(out=PF[:, PF_WK + 8:PF_WK + 16],
                         in_=PF[:, PF_WK + 8:PF_WK + 16])
            e.drain()
            e.tensor_mul(out=PF[:, PF_WK + 24:PF_WK + 32],
                         in0=g_v, in1=PF[:, PF_WK + 8:PF_WK + 16])
            e.drain()
            e.tensor_mul(out=PF[:, PF_WK + 16:PF_WK + 24],
                         in0=PF[:, PF_WK + 24:PF_WK + 32], in1=PF[:, PF_WK:PF_WK + 8])
            e.drain()
            beta = PF[:, pfv[("bq", "bk", "bv")[b]]:pfv[("bq", "bk", "bv")[b]] + 8]
            ins = e.tensor_sub(out=PF[:, PF_WK + 16:PF_WK + 24],
                               in0=beta, in1=PF[:, PF_WK + 16:PF_WK + 24])
            M.put(("bnw", b), s["c_v"].inc(ins))

    @blk.scalar
    def _(e):
        for b in range(3):
            s["c_v"].wait(e, M.get(("bnv", b)))
            ins = e.activation(out=PF[:, PF_WK + 8:PF_WK + 16],
                               in_=PF[:, PF_WK + 16:PF_WK + 24],
                               func=ACTF.Sqrt)
            M.put(("bnr", b), s["c_a"].inc(ins))
            s["c_v"].wait(e, M.get(("bnw", b)))
            e.activation(out=PF[:, PF_AC + b * 16:PF_AC + b * 16 + 8],
                         in_=PF[:, PF_WK + 24:PF_WK + 32], func=ACTF.Copy, scale=4.0)
            ins = e.activation(out=PF[:, PF_AC + b * 16 + 8:PF_AC + b * 16 + 16],
                               in_=PF[:, PF_WK + 16:PF_WK + 24],
                               func=ACTF.Copy, scale=4.0, bias=-4.0)
            M.put(("bnac", b), s["c_a"].inc(ins))

    @blk.sync
    def _(e):
        wait_bar(e, "p15end")
        for c in range(NC2):
            if c >= 2:
                s["c_a"].wait(e, M.get(("p2sg", c - 2)))
            ins = e.dma_start(out=FB[:, (c % 2) * 2048:(c % 2) * 2048 + 2048],
                              in_=h_sp.ap()[2, c, :, :])
            M.put(("p2hl", c), s["s_h2"].inc(ins, 16))

    @blk.scalar
    def _(e):
        for c in range(NC2):
            s["s_h2"].wait(e, M.get(("p2hl", c)))
            s["c_a"].wait(e, M.get(("bnac", 2)))
            if c >= 1:
                s["c_t"].wait(e, M.get(("p2tt", c - 1)))
            for et in range(ET):
                ins = e.activation(
                    out=FB[:, 8192 + et * C2: 8192 + (et + 1) * C2],
                    in_=FB[:, (c % 2) * 2048 + et * C2:(c % 2) * 2048 + (et + 1) * C2],
                    func=ACTF.Sigmoid,
                    scale=PF[:, PF_AC + 32 + et:PF_AC + 32 + et + 1],
                    bias=PF[:, PF_AC + 40 + et:PF_AC + 40 + et + 1])
            M.put(("p2sg", c), s["c_a"].inc(ins))
            for tq in range(2):
                nt = c * 2 + tq
                for half in range(2):
                    gidx = c * 4 + tq * 2 + half
                    bank = pb[4 + gidx % 2]
                    s["c_t"].wait(e, M.get(("p2t", gidx)))
                    ins = e.activation(
                        out=B1[:, nt * 1024 + half * 512: nt * 1024 + half * 512 + 512],
                        in_=bank[:], func=ACTF.Copy)
                    M.put(("p2va", gidx), s["c_a"].inc(ins))

    @blk.tensor
    def _(e):
        s["c_v"].wait(e, M.get(("bar", "p15end", "c_v")))
        for c in range(NC2):
            s["c_a"].wait(e, M.get(("p2sg", c)))
            for tq in range(2):
                nt = c * 2 + tq
                for half in range(2):
                    gidx = c * 4 + tq * 2 + half
                    bank = pb[4 + gidx % 2]
                    if gidx >= 2:
                        pg = gidx - 2
                        s["c_a"].wait(e, M.get(("p2va", pg)))
                        s["c_v"].wait(e, M.get(("p2vv", pg)))
                    for q in range(4):
                        et = half * 4 + q
                        ins = e.matmul(bank[:, q * P:(q + 1) * P],
                                       FB[:, 8192 + et * C2 + tq * P:
                                          8192 + et * C2 + tq * P + P],
                                       eyef, is_transpose=True,
                                       start=(q == 0), stop=(q == 3))
                    M.put(("p2t", gidx), s["c_t"].inc(e.drain()))
            M.put(("p2tt", c), s["c_t"].v)
            for tq in range(2):
                nt = c * 2 + tq
                g2 = c * 4 + tq * 2
                s["c_a"].wait(e, M.get(("p2va", g2 + 1)))
                s["c_v"].wait(e, M.get(("p2vv", g2 + 1)))
                for pr in range(8):
                    bs = B1[:, nt * 1024 + pr * P: nt * 1024 + (pr + 1) * P]
                    bl = B2[:, nt * 1024 + pr * P: nt * 1024 + (pr + 1) * P]
                    first = (nt == 0 and pr % 4 == 0)
                    last = (nt == NC2 * 2 - 1 and pr % 4 == 3)
                    e.matmul(pb[pr // 4][:, (pr % 4) * P:(pr % 4 + 1) * P],
                             bs, bs, start=first, stop=last)
                    ins = e.matmul(pb[2 + pr // 4][:, (pr % 4) * P:(pr % 4 + 1) * P],
                                   bs, bl, start=first, stop=last)
            M.put(("p2gm", c), s["c_t"].inc(e.drain()))

    @blk.vector
    def _(e):
        for c in range(NC2):
            for tq in range(2):
                nt = c * 2 + tq
                for half in range(2):
                    gidx = c * 4 + tq * 2 + half
                    bank = pb[4 + gidx % 2]
                    s["c_t"].wait(e, M.get(("p2t", gidx)))
                    s["c_a"].wait(e, M.get(("p2va", gidx)))
                    ins = e.scalar_tensor_tensor(
                        out=B2[:, nt * 1024 + half * 512: nt * 1024 + half * 512 + 512],
                        in0=bank[:], scalar=0.0,
                        in1=B1[:, nt * 1024 + half * 512: nt * 1024 + half * 512 + 512],
                        op0=ALU.add, op1=ALU.subtract)
                    M.put(("p2vv", gidx), s["c_v"].inc(ins))

    @blk.scalar
    def _(e):
        s["c_t"].wait(e, M.get(("p2gm", NC2 - 1)))
        e.activation(out=FB[:, 0:512], in_=pb[0][:], func=ACTF.Copy)
        e.activation(out=FB[:, 512:1024], in_=pb[1][:], func=ACTF.Copy)
        e.activation(out=FB[:, 1024:1536], in_=pb[2][:], func=ACTF.Copy)
        ins = e.activation(out=FB[:, 1536:2048], in_=pb[3][:], func=ACTF.Copy)
        M.put("vvev", s["c_a"].inc(ins))

    @blk.tensor
    def _(e):
        s["c_a"].wait(e, M.get("vvev"))
        for half in range(2):
            bank = pb[4 + half]
            for q in range(4):
                pr = half * 4 + q
                ins = e.matmul(bank[:, q * P:(q + 1) * P],
                               FB[:, 1024 + pr * P: 1024 + (pr + 1) * P],
                               eyef, is_transpose=True, start=(q == 0), stop=(q == 3))
        M.put("vvT", s["c_t"].inc(e.drain()))

    @blk.vector
    def _(e):
        s["c_t"].wait(e, M.get("vvT"))
        e.tensor_add(out=FB[:, 2048:3072], in0=FB[:, 0:1024], in1=FB[:, 1024:2048])
        e.tensor_add(out=FB[:, 3072:3584], in0=FB[:, 2048:2560], in1=pb[4][:])
        ins = e.tensor_add(out=FB[:, 3584:4096], in0=FB[:, 2560:3072], in1=pb[5][:])
        M.put("vvsym", s["c_v"].inc(ins))

    @blk.gpsimd
    def _(e):
        s["c_v"].wait(e, M.get("vvsym"))
        s["g_ar"].inc(e.dma_start(out=ar2_i.ap(), in_=FB[:, 3072:4096]), 16)
        e.wait_ge(s["g_ar"].h, s["g_ar"].v)
        s["cc"].inc(e.collective_compute(
            "AllReduce", ALU.add, replica_groups=[list(range(n_cores))],
            ins=[ar2_i.ap()], outs=[ar2_o.ap()]), 1)
        e.wait_ge(s["cc"].h, s["cc"].v)
        M.put("ar2_done", s["g_ar"].inc(
            e.dma_start(out=FB[:, 0:1024], in_=ar2_o.ap()), 16))

    @blk.vector
    def _(e):
        s["g_ar"].wait(e, M.get("ar2_done"))
        ins = e.memset(FB[:, 1024:2048], 0.0)
        M.put("vvz", s["c_v"].inc(ins))

    @blk.scalar
    def _(e):
        s["c_v"].wait(e, M.get("vvz"))
        for pr in range(8):
            e.activation(out=FB[0:64, 1024 + pr * P: 1024 + pr * P + 64],
                         in_=FB[0:64, pr * P: pr * P + 64],
                         func=ACTF.Copy, scale=0.125)
            ins = e.activation(out=FB[64:128, 1024 + pr * P + 64: 1024 + (pr + 1) * P],
                               in_=FB[64:128, pr * P + 64: pr * P + 64 + 64],
                               func=ACTF.Copy, scale=0.125)
        M.put("vvm", s["c_a"].inc(ins))
        e.drain()
        s["c_t"].wait(e, M.get(("q2t", ET - 1, 3)))
        ins = e.activation(out=SL[:, 4096:5120], in_=FB[:, 1024:2048], func=ACTF.Copy)
        M.put("vvhi", s["c_a"].inc(ins))

    @blk.vector
    def _(e):
        s["c_a"].wait(e, M.get("vvhi"))
        ins = e.scalar_tensor_tensor(out=SL[:, 5120:6144], in0=FB[:, 1024:2048],
                                     scalar=0.0, in1=SL[:, 4096:5120],
                                     op0=ALU.add, op1=ALU.subtract)
        M.put("vvlo", s["c_v"].inc(ins))

    @blk.sync
    def _(e):
        s["c_a"].wait(e, M.get(("p2sg", NC2 - 1)))
        for dt in range(ET):
            if dt >= 2:
                s["c_t"].wait(e, M.get(("wpT", dt - 2)))
            ins = e.dma_start(out=FB[:, 4096 + (dt % 2) * 1024: 4096 + (dt % 2) * 1024 + 1024],
                              in_=w_io["p"][dt * P:(dt + 1) * P, :])
            M.put(("wpl", dt), s["s_wst"].inc(ins, 16))

    @blk.tensor
    def _(e):
        s["c_t"].wait(e, M.get(("p2gm", NC2 - 1)))
        s["c_a"].wait(e, M.get(("wba2", ET - 1, 3, 1)))
        for dt in range(ET):
            s["s_wst"].wait(e, M.get(("wpl", dt)))
            for half in range(2):
                bank = pb[6 + half]
                if dt >= 1:
                    s["c_a"].wait(e, M.get(("wpa", dt - 1, half)))
                    s["c_v"].wait(e, M.get(("wpv", dt - 1, half)))
                for q in range(4):
                    kt = half * 4 + q
                    ins = e.matmul(bank[:, q * P:(q + 1) * P],
                                   FB[:, 4096 + (dt % 2) * 1024 + kt * P:
                                      4096 + (dt % 2) * 1024 + (kt + 1) * P],
                                   eyef, is_transpose=True, start=(q == 0), stop=(q == 3))
                M.put(("wpT2", dt, half), s["c_t"].inc(e.drain()))
            M.put(("wpT", dt), s["c_t"].v)

    @blk.scalar
    def _(e):
        for dt in range(ET):
            for half in range(2):
                bank = pb[6 + half]
                s["c_t"].wait(e, M.get(("wpT2", dt, half)))
                ins = e.activation(
                    out=B1[:, 8192 + dt * 1024 + half * 512:
                           8192 + dt * 1024 + half * 512 + 512],
                    in_=bank[:], func=ACTF.Copy)
                M.put(("wpa", dt, half), s["c_a"].inc(ins))

    @blk.vector
    def _(e):
        for dt in range(ET):
            for half in range(2):
                bank = pb[6 + half]
                s["c_t"].wait(e, M.get(("wpT2", dt, half)))
                s["c_a"].wait(e, M.get(("wpa", dt, half)))
                ins = e.scalar_tensor_tensor(
                    out=B2[:, 8192 + dt * 1024 + half * 512:
                           8192 + dt * 1024 + half * 512 + 512],
                    in0=bank[:], scalar=0.0,
                    in1=B1[:, 8192 + dt * 1024 + half * 512:
                           8192 + dt * 1024 + half * 512 + 512],
                    op0=ALU.add, op1=ALU.subtract)
                M.put(("wpv", dt, half), s["c_v"].inc(ins))

    @blk.vector
    def _(e):
        bar("p2end")

    @blk.sync
    def _(e):
        wait_bar(e, "p2end")
        for c in range(NC3):
            if c >= 1:
                s["c_a"].wait(e, M.get(("p3sg", c - 1)))
            ins = e.dma_start(out=FB[:, 0:2048], in_=h_sp.ap()[0, c, :, :])
            s["s_h3"].inc(ins, 16)
            ins = e.dma_start(out=FB[:, 2048:4096], in_=h_sp.ap()[1, c, :, :])
            M.put(("p3hl", c), s["s_h3"].inc(ins, 16))
            if c >= 1:
                s["c_v"].wait(e, M.get(("p3z", c - 1)))
            ins = e.dma_start(
                out=FB[:, 4096:6144].rearrange("p (tq d) -> p tq d", tq=2),
                in_=x_io[c * C3:(c + 1) * C3, :].rearrange("(tq p) d -> p tq d", p=P))
            M.put(("p3xl", c), s["s_xr"].inc(ins, 16))

    @blk.scalar
    def _(e):
        for c in range(NC3 + 1):
            pc = c - 1
            if c >= 1:
                s["c_t"].wait(e, M.get(("p3am", pc)))
                for j in range(4):
                    ins = e.activation(
                        out=B1[:, 4096 + (pc % 2) * 2048 + j * 512:
                               4096 + (pc % 2) * 2048 + j * 512 + 512],
                        in_=pb[4 + j][:], func=ACTF.Copy)
                M.put(("p3ah", pc), s["c_a"].inc(ins))
            if c < NC3:
                s["s_h3"].wait(e, M.get(("p3hl", c)))
                if DBG and c == 2:
                    s["g_out"].wait(e, M.get("dbg_done"))
                if c >= 1:
                    s["c_v"].wait(e, M.get(("p3qk", c - 1)))
                for et in range(ET):
                    e.activation(out=FB[:, 8192 + et * C3: 8192 + (et + 1) * C3],
                                 in_=FB[:, et * C3: (et + 1) * C3],
                                 func=ACTF.Sigmoid,
                                 scale=PF[:, PF_AC + et:PF_AC + et + 1],
                                 bias=PF[:, PF_AC + 8 + et:PF_AC + 8 + et + 1])
                    ins = e.activation(out=FB[:, 10240 + et * C3: 10240 + (et + 1) * C3],
                                       in_=FB[:, 2048 + et * C3: 2048 + (et + 1) * C3],
                                       func=ACTF.Sigmoid,
                                       scale=PF[:, PF_AC + 16 + et:PF_AC + 16 + et + 1],
                                       bias=PF[:, PF_AC + 24 + et:PF_AC + 24 + et + 1])
                M.put(("p3sg", c), s["c_a"].inc(ins))
                s["c_v"].wait(e, M.get(("p3m", c)))
                ins = e.activation(out=B1[:, (c % 2) * 2048:(c % 2) * 2048 + 2048],
                                   in_=FB[:, 10240:12288], func=ACTF.Copy)
                M.put(("p3qh", c), s["c_a"].inc(ins))
            if c >= 1:
                s["c_t"].wait(e, M.get(("p3ym", pc)))
                if DBG and pc == 1:
                    s["g_out"].wait(e, M.get("dbg_done"))
                    s["g_out"].wait(e, M.get("dbg6_done"))
                for dt in range(ET):
                    bank = pb[dt // 2]
                    off = (dt % 2) * C3
                    ins = e.activation(out=FB[:, 6144 + dt * C3: 6144 + (dt + 1) * C3],
                                       in_=bank[:, off:off + C3], func=ACTF.Identity,
                                       bias=PF[:, pfv["bp"] + dt:pfv["bp"] + dt + 1])
                M.put(("p3ye", pc), s["c_a"].inc(ins))
                M.put(("p3ya", pc), s["c_a"].v)

    @blk.vector
    def _(e):
        for c in range(NC3 + 1):
            pc = c - 1
            if c >= 1:
                s["c_t"].wait(e, M.get(("p3am", pc)))
                s["c_a"].wait(e, M.get(("p3ah", pc)))
                for j in range(4):
                    ins = e.scalar_tensor_tensor(
                        out=B2[:, 4096 + (pc % 2) * 2048 + j * 512:
                               4096 + (pc % 2) * 2048 + j * 512 + 512],
                        in0=pb[4 + j][:], scalar=0.0,
                        in1=B1[:, 4096 + (pc % 2) * 2048 + j * 512:
                               4096 + (pc % 2) * 2048 + j * 512 + 512],
                        op0=ALU.add, op1=ALU.subtract)
                M.put(("p3al", pc), s["c_v"].inc(ins))
            if c < NC3:
                s["c_a"].wait(e, M.get(("p3sg", c)))
                if c >= 2:
                    s["c_t"].wait(e, M.get(("p3aw", c - 2)))
                ins = e.tensor_mul(out=FB[:, 10240:12288],
                                   in0=FB[:, 8192:10240], in1=FB[:, 10240:12288])
                M.put(("p3m", c), s["c_v"].inc(ins))
                s["c_a"].wait(e, M.get(("p3qh", c)))
                ins = e.scalar_tensor_tensor(
                    out=B2[:, (c % 2) * 2048:(c % 2) * 2048 + 2048],
                    in0=FB[:, 10240:12288], scalar=0.0,
                    in1=B1[:, (c % 2) * 2048:(c % 2) * 2048 + 2048],
                    op0=ALU.add, op1=ALU.subtract)
                M.put(("p3qk", c), s["c_v"].inc(ins))
            if c >= 1:
                s["s_xr"].wait(e, M.get(("p3xl", pc)))
                for tq in range(2):
                    tt = pc * 2 + tq
                    for half in range(2):
                        bank = pb[tq * 2 + half]
                        s["c_t"].wait(e, M.get(("p3yt", pc, tq, half)))
                        ins = e.scalar_tensor_tensor(
                            out=FA[:, tt * 1024 + half * 512: tt * 1024 + half * 512 + 512],
                            in0=bank[:], scalar=0.0,
                            in1=FB[:, 4096 + tq * 1024 + half * 512:
                                   4096 + tq * 1024 + half * 512 + 512],
                            op0=ALU.add, op1=ALU.add,
                            accum_out=PF[:, PF_ZS + tt * 2 + half: PF_ZS + tt * 2 + half + 1])
                M.put(("p3z", pc), s["c_v"].inc(ins))

    @blk.tensor
    def _(e):
        if True:
            s["c_v"].wait(e, M.get("vvlo"))
            s["c_v"].wait(e, M.get(("wpv", ET - 1, 1)))
            s["c_a"].wait(e, M.get(("wpa", ET - 1, 1)))
        for c in range(NC3):
            s["c_v"].wait(e, M.get(("p3qk", c)))
            if c >= 1:
                s["c_v"].wait(e, M.get(("p3al", c - 1)))
            for pr in range(8):
                bank = pb[4 + pr // 2]
                off = (pr % 2) * C3
                qh = B1[:, (c % 2) * 2048 + pr * C3:(c % 2) * 2048 + (pr + 1) * C3]
                ql = B2[:, (c % 2) * 2048 + pr * C3:(c % 2) * 2048 + (pr + 1) * C3]
                vh = SL[:, 4096 + pr * P: 4096 + (pr + 1) * P]
                vl = SL[:, 5120 + pr * P: 5120 + (pr + 1) * P]
                first = (pr % 2 == 0)
                e.matmul(bank[:, off:off + C3], vh, qh, start=first, stop=False)
                e.matmul(bank[:, off:off + C3], vh, ql, start=False, stop=False)
                ins = e.matmul(bank[:, off:off + C3], vl, qh, start=False,
                               stop=(pr % 2 == 1))
            M.put(("p3am", c), s["c_t"].inc(e.drain()))
            s["c_a"].wait(e, M.get(("p3ah", c)))
            s["c_v"].wait(e, M.get(("p3al", c)))
            if c >= 1:
                s["c_v"].wait(e, M.get(("p3z", c - 1)))
            for dt in range(ET):
                bank = pb[dt // 2]
                off = (dt % 2) * C3
                for ip, (wl, al) in enumerate([(0, 0), (0, 1), (1, 0)]):
                    wsrc = B1 if wl == 0 else B2
                    asrc = B1 if al == 0 else B2
                    for kt in range(ET):
                        ins = e.matmul(
                            bank[:, off:off + C3],
                            wsrc[:, 8192 + dt * 1024 + kt * P: 8192 + dt * 1024 + (kt + 1) * P],
                            asrc[:, 4096 + (c % 2) * 2048 + kt * C3:
                                 4096 + (c % 2) * 2048 + (kt + 1) * C3],
                            start=(dt % 2 == 0 and ip == 0 and kt == 0),
                            stop=(dt % 2 == 1 and ip == 2 and kt == ET - 1))
            M.put(("p3ym", c), s["c_t"].inc(e.drain()))
            M.put(("p3aw", c), s["c_t"].v)
            s["c_a"].wait(e, M.get(("p3ya", c)))
            for tq in range(2):
                for half in range(2):
                    bank = pb[tq * 2 + half]
                    for q in range(4):
                        dt = half * 4 + q
                        ins = e.matmul(bank[:, q * P:(q + 1) * P],
                                       FB[:, 6144 + dt * C3 + tq * P:
                                          6144 + dt * C3 + tq * P + P],
                                       eyef, is_transpose=True,
                                       start=(q == 0), stop=(q == 3))
                    M.put(("p3yt", c, tq, half), s["c_t"].inc(e.drain()))

    if DBG:
        @blk.gpsimd
        def _(e):
            s["c_a"].wait(e, M.get(("p0a", NG0 - 1)))
            s["c_v"].wait(e, M.get(("p0v", NG0 - 1)))
            s["g_out"].inc(e.dma_start(out=dbgx.ap()[0], in_=B1[:, 0:2048]), 16)
            s["g_out"].inc(e.dma_start(out=dbgx.ap()[1], in_=B2[:, 0:2048]), 16)
            for b in range(3):
                s["g_h"].wait(e, M.get(("p1sp", b * ET, 3)))
                s["g_out"].inc(e.dma_start(out=dbgh.ap()[b],
                                           in_=h_sp.ap()[b, 0, :, :]), 16)
            s["g_wb"].wait(e, M.get("prep_done"))
            for j in range(4):
                s["g_out"].inc(e.dma_start(
                    out=dbgw.ap()[j],
                    in_=wqall_h.ap()[j // 2, :, (j % 2) * 1024:(j % 2) * 1024 + 1024]), 16)
            s["g_out"].inc(e.dma_start(
                out=dbgw.ap()[4:6].rearrange("f p c -> p f c"),
                in_=wqall_h.ap()[0:1, :, 2048:4096]
                    .rearrange("o p (f c) -> p (o f) c", f=2)), 16)
            s["c_v"].wait(e, M.get(("wpv", ET - 1, 1)))
            s["g_out"].inc(e.dma_start(out=dbgb.ap()[0][:, 0:1024],
                                       in_=B1[:, 11264:12288]), 16)
            s["g_out"].inc(e.dma_start(out=dbgb.ap()[1][:, 0:1024],
                                       in_=B2[:, 11264:12288]), 16)
            M.put("dbg5_done", s["g_out"].v)
            s["c_v"].wait(e, M.get(("p3qk", 0)))
            s["g_out"].inc(e.dma_start(out=dbgb.ap()[2], in_=B1[:, 0:2048]), 16)
            s["g_out"].inc(e.dma_start(out=dbgb.ap()[3], in_=B2[:, 0:2048]), 16)
            s["c_a"].wait(e, M.get(("p3ye", 0)))
            s["g_out"].inc(e.dma_start(out=dbgf.ap()[0], in_=FB[:, 6144:8192]), 16)
            M.put("dbg6_done", s["g_out"].v)
            s["c_v"].wait(e, M.get(("p3z", 0)))
            ins = e.dma_start(out=dbgf.ap()[1], in_=FA[:, 0:2048])
            M.put("dbg_done", s["g_out"].inc(ins, 16))
            s["c_v"].wait(e, M.get("p35app"))
            s["g_out"].inc(e.dma_start(out=dbgf.ap()[2], in_=FA[:, 0:2048]), 16)
            M.put("dbg2_done", s["g_out"].v)
            s["c_v"].wait(e, M.get(("p5z", 0)))
            ins = e.dma_start(out=dbgf.ap()[3], in_=FA[:, 0:2048])
            M.put("dbg3_done", s["g_out"].inc(ins, 16))
            s["c_v"].wait(e, M.get(("p5z", NC3 - 1)))
            s["g_out"].inc(e.dma_start(out=dbgf.ap()[4], in_=FA[:, (TT - 2) * 1024:]), 16)
            M.put("dbg4_done", s["g_out"].v)
            s["c_a"].wait(e, M.get("p6r"))
            s["g_out"].inc(e.dma_start(out=dbgc.ap(), in_=PF[:, :]), 16)

    @blk.vector
    def _(e):
        bar("p3end")

    @blk.scalar
    def _(e):
        wait_bar(e, "p3end")
        for tt in range(TT):
            ins = e.activation(out=FB[:, 0:1024], in_=FA[:, tt * 1024:(tt + 1) * 1024],
                               func=ACTF.Square,
                               accum_out=PF[:, PF_Z2 + tt:PF_Z2 + tt + 1])
        M.put("p35sq", s["c_a"].inc(ins))

    @blk.vector
    def _(e):
        s["c_a"].wait(e, M.get("p35sq"))
        ins = e.tensor_reduce(
            out=PF[:, PF_WK:PF_WK + 16].rearrange("p (t o) -> p t o", o=1),
            in_=PF[:, PF_ZS:PF_ZS + 32].rearrange("p (t n) -> p t n", n=2),
            axis=AXX, op=ALU.add)
        s["c_v"].inc(ins)
        e.drain()
        e.tensor_scalar(out=PF[:, PF_M:PF_M + 16], in0=PF[:, PF_WK:PF_WK + 16],
                        scalar1=1.0 / D, scalar2=None, op0=ALU.mult)
        e.tensor_scalar(out=PF[:, PF_WK:PF_WK + 16], in0=PF[:, PF_Z2:PF_Z2 + 16],
                        scalar1=1.0 / D, scalar2=None, op0=ALU.mult)
        e.drain()
        e.tensor_mul(out=PF[:, PF_WK + 16:PF_WK + 32],
                     in0=PF[:, PF_M:PF_M + 16], in1=PF[:, PF_M:PF_M + 16])
        e.drain()
        ins = e.scalar_tensor_tensor(out=PF[:, PF_WK:PF_WK + 16],
                                     in0=PF[:, PF_WK:PF_WK + 16],
                                     scalar=LN_EPS, op0=ALU.add,
                                     in1=PF[:, PF_WK + 16:PF_WK + 32],
                                     op1=ALU.subtract)
        M.put("p35v", s["c_v"].inc(ins))

    @blk.scalar
    def _(e):
        s["c_v"].wait(e, M.get("p35v"))
        ins = e.activation(out=PF[:, PF_R:PF_R + 16], in_=PF[:, PF_WK:PF_WK + 16],
                           func=ACTF.Sqrt)
        M.put("p35r", s["c_a"].inc(ins))

    @blk.vector
    def _(e):
        s["c_a"].wait(e, M.get("p35r"))
        if DBG:
            s["g_out"].wait(e, M.get("dbg_done"))
        e.reciprocal(out=PF[:, PF_R:PF_R + 16], in_=PF[:, PF_R:PF_R + 16])
        e.drain()
        for tt in range(TT):
            ins = e.tensor_scalar(out=FA[:, tt * 1024:(tt + 1) * 1024],
                                  in0=FA[:, tt * 1024:(tt + 1) * 1024],
                                  scalar1=PF[:, PF_M + tt:PF_M + tt + 1],
                                  scalar2=PF[:, PF_R + tt:PF_R + tt + 1],
                                  op0=ALU.subtract, op1=ALU.mult)
            M.put(("p35a", tt), s["c_v"].inc(ins))
        M.put("p35app", s["c_v"].v)

    @blk.tensor
    def _(e):
        for g in range(NG0):
            kt, j = divmod(g, TT // 4)
            bank = pb[g % 2]
            for q in range(4):
                tt = 4 * j + q
                s["c_v"].wait(e, M.get(("p35a", tt)))
            if g >= 2:
                s["c_a"].wait(e, M.get(("x1a", g - 2)))
                s["c_v"].wait(e, M.get(("x1v", g - 2)))
            for q in range(4):
                tt = 4 * j + q
                ins = e.matmul(bank[:, q * P:(q + 1) * P],
                               FA[:, tt * 1024 + kt * P: tt * 1024 + (kt + 1) * P],
                               eyef, is_transpose=True, start=(q == 0), stop=(q == 3))
            M.put(("x1t", g), s["c_t"].inc(e.drain()))

    @blk.scalar
    def _(e):
        for g in range(NG0):
            kt, j = divmod(g, TT // 4)
            bank = pb[g % 2]
            s["c_t"].wait(e, M.get(("x1t", g)))
            if DBG and g == 0:
                s["g_out"].wait(e, M.get("dbg5_done"))
            ins = e.activation(out=B1[:, kt * T + 4 * j * P: kt * T + (4 * j + 4) * P],
                               in_=bank[:], func=ACTF.Copy)
            M.put(("x1a", g), s["c_a"].inc(ins))

    @blk.vector
    def _(e):
        for g in range(NG0):
            kt, j = divmod(g, TT // 4)
            bank = pb[g % 2]
            s["c_t"].wait(e, M.get(("x1t", g)))
            s["c_a"].wait(e, M.get(("x1a", g)))
            ins = e.scalar_tensor_tensor(
                out=B2[:, kt * T + 4 * j * P: kt * T + (4 * j + 4) * P],
                in0=bank[:], scalar=0.0,
                in1=B1[:, kt * T + 4 * j * P: kt * T + (4 * j + 4) * P],
                op0=ALU.add, op1=ALU.subtract)
            M.put(("x1v", g), s["c_v"].inc(ins))

    @blk.vector
    def _(e):
        bar("p35end")

    @blk.sync
    def _(e):
        wait_bar(e, "p35end", extra=[("g_wb", M.get("prep_done"))])
        for c in range(NC3):
            for fc in range(NFC):
                gi = c * NFC + fc
                if gi >= 3:
                    s["c_t"].wait(e, M.get(("f1m", gi - 3)))
                    s["c_t"].wait(e, M.get(("f2done", gi - 3)))
                ins = e.dma_start(out=SL[:, (gi % 3) * 4096:(gi % 3) * 4096 + 4096],
                                  in_=wqall_h.ap()[fc, :, :])
                M.put(("fsl", gi), s["s_fsl"].inc(ins, 16))

    @blk.tensor
    def _(e):
        wait_bar(e, "p35end")
        for c in range(NC3):
            for fc in range(NFC + 1):
                gi = c * NFC + fc
                if fc < NFC:
                    bank = pb[4 + fc % 2]
                    s["s_fsl"].wait(e, M.get(("fsl", gi)))
                    if gi >= 2:
                        s["c_v"].wait(e, M.get(("f1r", gi - 2)))
                    if fc <= 1 and c >= 1:
                        s["c_v"].wait(e, M.get(("p5z", c - 1)))
                    for fcl in range(2):
                        for xl in range(2):
                            xs = B1 if xl == 0 else B2
                            for dt in range(ET):
                                ins = e.matmul(
                                    bank[:, fcl * C3:(fcl + 1) * C3],
                                    SL[:, (gi % 3) * 4096 + fcl * 1024 + dt * P:
                                       (gi % 3) * 4096 + fcl * 1024 + (dt + 1) * P],
                                    xs[:, dt * T + c * C3: dt * T + (c + 1) * C3],
                                    start=(fcl == 0 and xl == 0 and dt == 0),
                                    stop=(fcl == 1 and xl == 1 and dt == ET - 1))
                    M.put(("f1m", gi), s["c_t"].inc(e.drain()))
                if fc >= 1:
                    pfc = fc - 1
                    pgi = c * NFC + pfc
                    s["c_a"].wait(e, M.get(("f1h", pgi)))
                    s["c_v"].wait(e, M.get(("f1l", pgi)))
                    if fc == 1 and c >= 1:
                        s["c_a"].wait(e, M.get(("p5ye", c - 1)))
                    for dt in range(ET):
                        bank = pb[dt // 2]
                        off = (dt % 2) * C3
                        for hl in range(2):
                            hsrc_base = PB_H1H if hl == 0 else PB_H1L
                            for fcl in range(2):
                                ins = e.matmul(
                                    bank[:, off:off + C3],
                                    SL[:, (pgi % 3) * 4096 + 2048 + fcl * 1024 + dt * P:
                                       (pgi % 3) * 4096 + 2048 + fcl * 1024 + (dt + 1) * P],
                                    PB[:, hsrc_base + (pfc % 2) * 512 + fcl * C3:
                                       hsrc_base + (pfc % 2) * 512 + (fcl + 1) * C3],
                                    start=(pfc == 0 and dt % 2 == 0 and hl == 0 and fcl == 0),
                                    stop=(pfc == NFC - 1 and dt % 2 == 1 and hl == 1 and fcl == 1))
                    if pfc == NFC - 1:
                        M.put(("f2done", pgi), s["c_t"].inc(e.drain()))
                        M.put(("f2last", c), s["c_t"].v)
                    else:
                        M.put(("f2done", pgi), s["c_t"].inc(ins))
            for tq in range(2):
                for half in range(2):
                    bank = pb[4 + tq * 2 + half]
                    if tq == 0 and half == 0:
                        s["c_a"].wait(e, M.get(("p5ye", c)))
                        s["c_v"].wait(e, M.get(("f1r", c * NFC + NFC - 1)))
                    for q in range(4):
                        dt = half * 4 + q
                        ins = e.matmul(bank[:, q * P:(q + 1) * P],
                                       FB[:, 1024 + dt * C3 + tq * P:
                                          1024 + dt * C3 + tq * P + P],
                                       eyef, is_transpose=True,
                                       start=(q == 0), stop=(q == 3))
                    M.put(("p5yt", c, tq, half), s["c_t"].inc(e.drain()))

    @blk.vector
    def _(e):
        for c in range(NC3):
            for fc in range(NFC):
                gi = c * NFC + fc
                bank = pb[4 + fc % 2]
                s["c_t"].wait(e, M.get(("f1m", gi)))
                if gi >= 2:
                    s["c_a"].wait(e, M.get(("f1h", gi - 2)))
                for fcl in range(2):
                    ins = e.tensor_scalar(
                        out=FB[:, (fc % 2) * 512 + fcl * C3:(fc % 2) * 512 + (fcl + 1) * C3],
                        in0=bank[:, fcl * C3:(fcl + 1) * C3],
                        scalar1=PF[:, PF_B1V + fc * 2 + fcl:PF_B1V + fc * 2 + fcl + 1],
                        scalar2=0.0, op0=ALU.add, op1=ALU.max)
                M.put(("f1r", gi), s["c_v"].inc(ins))
                s["c_a"].wait(e, M.get(("f1h", gi)))
                ins = e.scalar_tensor_tensor(
                    out=PB[:, PB_H1L + (fc % 2) * 512: PB_H1L + (fc % 2) * 512 + 512],
                    in0=FB[:, (fc % 2) * 512:(fc % 2) * 512 + 512], scalar=0.0,
                    in1=PB[:, PB_H1H + (fc % 2) * 512: PB_H1H + (fc % 2) * 512 + 512],
                    op0=ALU.add, op1=ALU.subtract)
                M.put(("f1l", gi), s["c_v"].inc(ins))
            if DBG and c == 0:
                s["g_out"].wait(e, M.get("dbg2_done"))
            for tq in range(2):
                tt = c * 2 + tq
                for half in range(2):
                    bank = pb[4 + tq * 2 + half]
                    s["c_t"].wait(e, M.get(("p5yt", c, tq, half)))
                    ins = e.scalar_tensor_tensor(
                        out=FA[:, tt * 1024 + half * 512: tt * 1024 + half * 512 + 512],
                        in0=bank[:], scalar=0.0,
                        in1=FA[:, tt * 1024 + half * 512: tt * 1024 + half * 512 + 512],
                        op0=ALU.add, op1=ALU.add,
                        accum_out=PF[:, PF_ZS + tt * 2 + half: PF_ZS + tt * 2 + half + 1])
            M.put(("p5z", c), s["c_v"].inc(ins))

    @blk.scalar
    def _(e):
        for c in range(NC3):
            for fc in range(NFC):
                gi = c * NFC + fc
                s["c_v"].wait(e, M.get(("f1r", gi)))
                if gi >= 2:
                    s["c_t"].wait(e, M.get(("f2done", gi - 2)))
                ins = e.activation(
                    out=PB[:, PB_H1H + (fc % 2) * 512: PB_H1H + (fc % 2) * 512 + 512],
                    in_=FB[:, (fc % 2) * 512:(fc % 2) * 512 + 512], func=ACTF.Copy)
                M.put(("f1h", gi), s["c_a"].inc(ins))
            s["c_t"].wait(e, M.get(("f2last", c)))
            for dt in range(ET):
                bank = pb[dt // 2]
                off = (dt % 2) * C3
                ins = e.activation(out=FB[:, 1024 + dt * C3: 1024 + (dt + 1) * C3],
                                   in_=bank[:, off:off + C3], func=ACTF.Identity,
                                   bias=PF[:, pfv["b2v"] + dt:pfv["b2v"] + dt + 1])
            M.put(("p5ye", c), s["c_a"].inc(ins))

    @blk.vector
    def _(e):
        bar("p5end")

    @blk.scalar
    def _(e):
        wait_bar(e, "p5end")
        for tt in range(TT):
            ins = e.activation(out=FB[:, 2048 + 1024 * (tt % 2): 2048 + 1024 * (tt % 2) + 1024],
                               in_=FA[:, tt * 1024:(tt + 1) * 1024],
                               func=ACTF.Square,
                               accum_out=PF[:, PF_Z2 + tt:PF_Z2 + tt + 1])
        M.put("p6sq", s["c_a"].inc(ins))

    @blk.vector
    def _(e):
        s["c_a"].wait(e, M.get("p6sq"))
        ins = e.tensor_reduce(
            out=PF[:, PF_WK:PF_WK + 16].rearrange("p (t o) -> p t o", o=1),
            in_=PF[:, PF_ZS:PF_ZS + 32].rearrange("p (t n) -> p t n", n=2),
            axis=AXX, op=ALU.add)
        s["c_v"].inc(ins)
        e.drain()
        e.tensor_scalar(out=PF[:, PF_M:PF_M + 16], in0=PF[:, PF_WK:PF_WK + 16],
                        scalar1=1.0 / D, scalar2=None, op0=ALU.mult)
        e.tensor_scalar(out=PF[:, PF_WK:PF_WK + 16], in0=PF[:, PF_Z2:PF_Z2 + 16],
                        scalar1=1.0 / D, scalar2=None, op0=ALU.mult)
        e.drain()
        e.tensor_mul(out=PF[:, PF_WK + 16:PF_WK + 32],
                     in0=PF[:, PF_M:PF_M + 16], in1=PF[:, PF_M:PF_M + 16])
        e.drain()
        ins = e.scalar_tensor_tensor(out=PF[:, PF_WK:PF_WK + 16],
                                     in0=PF[:, PF_WK:PF_WK + 16],
                                     scalar=LN_EPS, op0=ALU.add,
                                     in1=PF[:, PF_WK + 16:PF_WK + 32],
                                     op1=ALU.subtract)
        M.put("p6v", s["c_v"].inc(ins))

    @blk.scalar
    def _(e):
        s["c_v"].wait(e, M.get("p6v"))
        ins = e.activation(out=PF[:, PF_R:PF_R + 16], in_=PF[:, PF_WK:PF_WK + 16],
                           func=ACTF.Sqrt)
        M.put("p6r", s["c_a"].inc(ins))

    @blk.vector
    def _(e):
        s["c_a"].wait(e, M.get("p6r"))
        if DBG:
            s["g_out"].wait(e, M.get("dbg3_done"))
            s["g_out"].wait(e, M.get("dbg4_done"))
        e.reciprocal(out=PF[:, PF_R:PF_R + 16], in_=PF[:, PF_R:PF_R + 16])
        e.drain()
        for tt in range(TT):
            ins = e.tensor_scalar(out=FA[:, tt * 1024:(tt + 1) * 1024],
                                  in0=FA[:, tt * 1024:(tt + 1) * 1024],
                                  scalar1=PF[:, PF_M + tt:PF_M + tt + 1],
                                  scalar2=PF[:, PF_R + tt:PF_R + tt + 1],
                                  op0=ALU.subtract, op1=ALU.mult)
            M.put(("p6a", tt), s["c_v"].inc(ins))

    @blk.gpsimd
    def _(e):
        for tt in range(TT):
            s["c_v"].wait(e, M.get(("p6a", tt)))
            ins = e.dma_start(out=out_io[tt * P:(tt + 1) * P, :],
                              in_=FA[:, tt * 1024:(tt + 1) * 1024])
            s["g_out"].inc(ins, 16)
        e.wait_ge(s["g_out"].h, s["g_out"].v)

    stack.close()
    return nc


_BUILD_CACHE = {}


def _get_nc(n_cores):
    if n_cores not in _BUILD_CACHE:
        marks = Marks()
        build_nc(n_cores, marks)
        marks.check = True
        _BUILD_CACHE[n_cores] = build_nc(n_cores, marks)
    return _BUILD_CACHE[n_cores]


def make_inputs(L0_attn_x, L0_attn_Wq, L0_attn_gq, L0_attn_bq, L0_attn_Wk,
                L0_attn_gk, L0_attn_bk, L0_attn_Wv, L0_attn_gv, L0_attn_bv,
                L0_attn_Wp, L0_attn_bp, L0_ffn_W1, L0_ffn_b1, L0_ffn_W2,
                L0_ffn_b2, n_cores):
    f32 = lambda a: np.ascontiguousarray(np.asarray(a, dtype=np.float32))
    vec = lambda v, r: np.ascontiguousarray(
        np.asarray(v, dtype=np.float32).reshape(r, P).T)
    x = f32(L0_attn_x).reshape(-1, D)
    shards = np.split(x, n_cores, axis=0)
    common = {
        "wq": f32(L0_attn_Wq), "wk": f32(L0_attn_Wk), "wv": f32(L0_attn_Wv),
        "wp": f32(L0_attn_Wp),
        "w1": f32(L0_ffn_W1), "w2": f32(L0_ffn_W2),
        "gq": vec(L0_attn_gq, 8), "bq": vec(L0_attn_bq, 8),
        "gk": vec(L0_attn_gk, 8), "bk": vec(L0_attn_bk, 8),
        "gv": vec(L0_attn_gv, 8), "bv": vec(L0_attn_bv, 8),
        "bp": vec(L0_attn_bp, 8), "b2v": vec(L0_ffn_b2, 8),
        "b1v": vec(L0_ffn_b1, 32),
        "eyef": np.eye(P, dtype=np.float32),
        "onesq": np.ones((P, P), dtype=np.float32),
        "eyeb": np.eye(P, dtype=ml_dtypes.bfloat16),
        "ones128": np.ones((P, 1), dtype=np.float32),
    }
    return [dict(common, x=np.ascontiguousarray(sh)) for sh in shards]


def kernel(**inputs):
    n_cores = N_CORES
    in_maps = make_inputs(n_cores=n_cores, **inputs)
    nc = _get_nc(n_cores)
    res = run_bass_kernel_spmd(nc, in_maps, core_ids=list(range(n_cores)))
    out = np.concatenate([res.results[c]["out"] for c in range(n_cores)], axis=0)
    return out.reshape(1, n_cores * T, D)
